# version 09 (final)
# baseline (speedup 1.0000x reference)
import math
from contextlib import ExitStack

import numpy as np
import ml_dtypes

import concourse.bass as bass
import concourse.mybir as mybir
from concourse.bass_utils import run_bass_kernel_spmd

F32 = mybir.dt.float32
BF16 = mybir.dt.bfloat16
AF = mybir.ActivationFunctionType
ALU = mybir.AluOpType
AXX = mybir.AxisListType.X
BF = ml_dtypes.bfloat16

D = 1024
NCORE = 8

ENGINES = ("sync", "tensor", "vector", "scalar", "gpsimd")


class Prog:
    ENG2SEM = {"tensor": "pe", "vector": "dve", "scalar": "act", "gpsimd": "pool"}
    DMACLASS = {"sync": "hw", "scalar": "hw", "gpsimd": "sw"}

    def __init__(self, nc):
        self.nc = nc
        self.es = ExitStack()
        self.ops = {e: [] for e in ENGINES}
        self.sem = {}
        for name in ("pe", "dve", "act", "pool"):
            self.sem[name] = self.es.enter_context(nc.semaphore(name))
        self.cnt = {name: 0 for name in self.sem}
        self.seen = {e: {} for e in ENGINES}
        self.dmaclass = {}
        self.writer = {}
        self.readers = {}
        self.pending = {e: {"r": [], "w": []} for e in ENGINES}
        self.batch_sems = set()
        self.n_ops = 0

    BATCHY = ("ld", "p2c", "cl", "cg", "cvcs", "cvldd", "ffndb", "dconst",
              "ptw", "lpwgk", "ffnw", "xnt")

    def _is_batchy(self, name):
        if name in ("lnst",):
            return False
        return any(b in name for b in self.BATCHY)

    def dma_sem(self, name, klass):
        if self._is_batchy(name):
            self.batch_sems.add(name)
        if name not in self.sem:
            self.sem[name] = self.es.enter_context(self.nc.semaphore(name))
            self.cnt[name] = 0
            self.dmaclass[name] = klass
        else:
            assert self.dmaclass.get(name, klass) == klass, \
                f"dma sem {name} shared between HWDGE and SWDGE"
        return self.sem[name]

    def sb(self, name, shape, dtype, scope=None):
        cm = self.nc.sbuf_tensor(name, list(shape), dtype)
        return (scope or self.es).enter_context(cm)

    def ps(self, name, shape, dtype=F32, scope=None):
        cm = self.nc.psum_tensor(name, list(shape), dtype)
        return (scope or self.es).enter_context(cm)

    def _wait(self, engine, sem_name, val):
        if sem_name in self.batch_sems:
            val = self.cnt[sem_name]
        if val <= 0 or self.seen[engine].get(sem_name, 0) >= val:
            return
        self.seen[engine][sem_name] = val
        self.ops[engine].append(("w", self.sem[sem_name], val))

    def wait_key(self, engine, key):
        w = self.writer.get(key)
        if w is not None:
            self._wait(engine, w[0], w[1])

    def op(self, engine, fn, reads=(), writes=(), war=(), wlast=(), dma=False,
           inc=True):
        for key in reads:
            self.wait_key(engine, key)
        for key in tuple(writes) + tuple(war):
            self.wait_key(engine, key)
            for r in self.readers.get(key, ()):
                self._wait(engine, r[0], r[1])
        self.n_ops += 1
        if not inc:
            assert not dma
            self.ops[engine].append(("n", fn))
            self.pending[engine]["r"] += list(reads)
            self.pending[engine]["w"] += list(writes) + list(wlast)
            return None
        if dma:
            assert isinstance(dma, str), "dma= must be a semaphore name"
            self.dma_sem(dma, self.DMACLASS[engine])
            sem_name, n = dma, 16
        else:
            sem_name, n = self.ENG2SEM[engine], 1
        self.cnt[sem_name] += n
        tok = (sem_name, self.cnt[sem_name])
        self.ops[engine].append(("o", fn, self.sem[sem_name], n))
        pend = self.pending[engine]
        wkeys = list(writes) + list(wlast) + pend["w"]
        rkeys = list(reads) + pend["r"]
        pend["r"], pend["w"] = [], []
        for key in wkeys:
            self.writer[key] = tok
            self.readers[key] = []
        for key in rkeys:
            self.readers.setdefault(key, []).append(tok)
        return tok

    def barrier(self, engines=ENGINES):
        for e in engines:
            assert not self.pending[e]["r"] and not self.pending[e]["w"], \
                f"pending non-inc ops on {e} at barrier"
            for s, c in self.cnt.items():
                self._wait(e, s, c)

    def finalize(self):
        for e in ENGINES:
            assert not self.pending[e]["r"] and not self.pending[e]["w"]
        with self.es:
            with self.nc.Block() as block:
                for engine in ENGINES:
                    ops = self.ops[engine]
                    if not ops:
                        continue

                    def body(eng, ops=ops):
                        for o in ops:
                            if o[0] == "w":
                                eng.wait_ge(o[1], o[2])
                            elif o[0] == "n":
                                o[1](eng)
                            else:
                                o[1](eng).then_inc(o[2], o[3])
                    getattr(block, engine)(body)
        return self.nc



def emit_consts(p, nc):
    c = {}
    c["identb"] = p.sb("c_identb", [128, 128], BF16)
    ib = nc.dram_tensor("identb", [128, 128], BF16, kind="ExternalInput")
    p.op("sync", lambda s: s.dma_start(out=c["identb"][:, :], in_=ib[:, :]),
         writes=["identb"], dma="dconst")
    c["identf"] = p.sb("c_identf", [128, 128], F32)
    i2 = nc.dram_tensor("identf", [128, 128], F32, kind="ExternalInput")
    p.op("sync", lambda s: s.dma_start(out=c["identf"][:, :], in_=i2[:, :]),
         writes=["identf"], dma="dconst")
    c["eps"] = p.sb("c_eps", [128, 1], F32)
    p.op("gpsimd", lambda g: g.memset(c["eps"][:, :], 1e-5), writes=["eps"])
    c["m32"] = p.sb("c_m32", [128, 1], F32)
    p.op("gpsimd", lambda g: g.memset(c["m32"][:, :], -32.0), writes=["m32"])
    c["onesb"] = p.sb("c_onesb", [1, 512], BF16)
    p.op("gpsimd", lambda g: g.memset(c["onesb"][:, :], 1.0), writes=["onesb"])
    return c


def emit_ln_fm(p, x_hbm, T, xn_fm, c, tagp, xnfm_hbm=None, xnkey="xnfm"):
    with ExitStack() as sc:
        xt = [p.sb(f"{tagp}_xt{j}", [128, D], F32, sc) for j in range(2)]
        xn = [p.sb(f"{tagp}_xn{j}", [128, D], BF16, sc) for j in range(2)]
        st = [p.sb(f"{tagp}_st{j}", [128, 4], F32, sc) for j in range(2)]
        sq = [p.sb(f"{tagp}_sq0", [128, D], F32, sc)] * 2
        ptr = [p.ps(f"{tagp}_ptr{j}", [128, 128], BF16, sc) for j in range(2)]
        bnc = [p.sb(f"{tagp}_bnc{j}", [128, 256], BF16, sc) for j in range(2)] \
            if xn_fm is None else None
        for i in range(T // 128):
            j = i % 2
            kx, kn, ks = (tagp, "x", j), (tagp, "n", j), (tagp, "s", j)
            kq, kp = (tagp, "q", j), (tagp, "p", j)
            p.op("sync", lambda s, i=i, j=j: s.dma_start(
                out=xt[j][:, :], in_=x_hbm[i * 128:(i + 1) * 128, :]),
                writes=[kx], dma=f"lnx{j}")
            p.op("vector", lambda v, j=j: v.reduce_sum(
                st[j][:, 0:1], xt[j][:, :], axis=AXX), reads=[kx], writes=[ks])
            p.op("vector", lambda v, j=j: v.tensor_scalar_mul(
                st[j][:, 0:1], st[j][:, 0:1], 1.0 / D), reads=[ks], writes=[ks])
            p.op("vector", lambda v, j=j: v.tensor_scalar_sub(
                xt[j][:, :], xt[j][:, :], st[j][:, 0:1]), reads=[kx, ks], writes=[kx])
            p.op("scalar", lambda s, j=j: s.activation(
                sq[j][:, :], xt[j][:, :], AF.Square, accum_out=st[j][:, 1:2]),
                reads=[kx], writes=[(tagp, "sq"), ks])
            p.op("scalar", lambda s, j=j: s.activation(
                st[j][:, 2:3], st[j][:, 1:2], AF.Ln, bias=c["eps"][:, 0:1],
                scale=1.0 / D), reads=[ks, "eps"], writes=[ks])
            p.op("scalar", lambda s, j=j: s.activation(
                st[j][:, 2:3], st[j][:, 2:3], AF.Exp, scale=-0.5),
                reads=[ks], writes=[ks])
            p.op("vector", lambda v, j=j: v.tensor_scalar_mul(
                xn[j][:, :], xt[j][:, :], st[j][:, 2:3]), reads=[kx, ks], writes=[kn])
            for db in range(8):
                p.op("tensor", lambda t, j=j, db=db: t.transpose(
                    ptr[j][:, :], xn[j][:, db * 128:(db + 1) * 128],
                    c["identb"][:, :]),
                    reads=[kn, "identb"], writes=[kp])
                if xn_fm is not None:
                    p.op("vector", lambda v, j=j, db=db, i=i: v.tensor_copy(
                        xn_fm[:, db * T + i * 128: db * T + (i + 1) * 128],
                        ptr[j][:, :]),
                        reads=[kp], wlast=[(xnkey, db)])
                    if xnfm_hbm is not None:
                        p.op("sync", lambda s, db=db, i=i: s.dma_start(
                            out=xnfm_hbm[db * 128:(db + 1) * 128,
                                         i * 128:(i + 1) * 128],
                            in_=xn_fm[:, db * T + i * 128: db * T + (i + 1) * 128]),
                            reads=[(xnkey, db)], dma="lnst")
                else:
                    bk = (tagp, "bn", j, db % 2)
                    p.op("vector", lambda v, j=j, db=db: v.tensor_copy(
                        bnc[j][:, (db % 2) * 128:(db % 2) * 128 + 128],
                        ptr[j][:, :]),
                        reads=[kp], writes=[bk])
                    p.op("sync", lambda s, db=db, i=i, j=j: s.dma_start(
                        out=xnfm_hbm[db * 128:(db + 1) * 128, i * 128:(i + 1) * 128],
                        in_=bnc[j][:, (db % 2) * 128:(db % 2) * 128 + 128]),
                        reads=[bk], dma=f"lnb{j}{db % 2}")
        p.barrier()


def emit_load_xnfm(p, xnfm_hbm, T, xn_fm, xnkey="xnfm"):
    for db in range(8):
        p.op("sync", lambda s, db=db: s.dma_start(
            out=xn_fm[:, db * T:(db + 1) * T],
            in_=xnfm_hbm[db * 128:(db + 1) * 128, :]),
            writes=[(xnkey, db)], dma="dxn")


def emit_proj_tm(p, w_hbm, n_cols, xn_fm, T, out_cb, tagp, c,
                 bias_hbm=None, pp=2, xnkey="xnfm"):
    NB = n_cols // 512
    assert NB <= 3
    with ExitStack() as sc:
        p.batch_sems.add("ptw")
        wf = p.sb(f"{tagp}_wf", [128, 8 * n_cols], BF16, sc)
        pt = [p.ps(f"{tagp}_pt{j}", [128, 512], F32, sc) for j in range(pp * NB)]
        for k in range(8):
            p.op("gpsimd", lambda g, k=k: g.dma_start(
                out=wf[:, k * n_cols:(k + 1) * n_cols],
                in_=w_hbm[k * 128:(k + 1) * 128, :]),
                writes=[(tagp, "w", k)], dma="ptw")
        if bias_hbm is not None:
            brow = p.sb(f"{tagp}_bias", [1, n_cols], BF16, sc)
            p.op("gpsimd", lambda g: g.dma_start(out=brow[:, :], in_=bias_hbm[:, :]),
                 writes=[(tagp, "bias")], dma="ptw")
        for tb in range(T // 128):
            g = tb % pp
            for nb in range(NB):
                ps = pt[g * NB + nb]
                key = (tagp, "pt", g, nb)
                nmm = 8 + (1 if bias_hbm is not None else 0)
                for k in range(8):
                    last = (k == 7 and bias_hbm is None)
                    p.op("tensor", lambda t, tb=tb, k=k, nb=nb, ps=ps, last=last: t.matmul(
                        ps[:, :],
                        xn_fm[:, k * T + tb * 128: k * T + (tb + 1) * 128],
                        wf[:, k * n_cols + nb * 512: k * n_cols + (nb + 1) * 512],
                        start=(k == 0), stop=last),
                        reads=[(xnkey, k), (tagp, "w", k)],
                        writes=[key] if last else ([] if k else []),
                        war=[key] if k == 0 else (),
                        inc=last)
                if bias_hbm is not None:
                    p.op("tensor", lambda t, nb=nb, ps=ps: t.matmul(
                        ps[:, :], c["onesb"][0:1, 0:128],
                        brow[0:1, nb * 512:(nb + 1) * 512],
                        start=False, stop=True),
                        reads=[(tagp, "bias"), "onesb"], writes=[key], inc=True)
                out_cb(p, ps, tb, nb, key)
        p.barrier()



def _f32c(x):
    return np.ascontiguousarray(np.asarray(x, dtype=np.float32))


def host_consts():
    return {"identb": np.eye(128, dtype=np.float32).astype(BF)}


def emit_l0_proj_tm(p, ws, xn_fm, T, hbm, c, tag="l0p", xnkey="xnfm"):
    nc = p.nc
    with ExitStack() as sc:
        ob = [p.sb(f"{tag}_ob{j}", [128, 512], BF16, sc) for j in range(6)]
        gkx = [p.sb(f"{tag}_gkx{j}", [128, 16], BF16, sc) for j in range(2)]
        gkxT = [p.ps(f"{tag}_gkxT{j}", [16, 128], BF16, sc) for j in range(2)]
        gkxTs = [p.sb(f"{tag}_gkxTs{j}", [16, 128], BF16, sc) for j in range(2)]
        wgk2 = p.sb(f"{tag}_wgk2", [16, 512], BF16, sc)
        bgk2 = p.sb(f"{tag}_bgk2", [1, 512], BF16, sc)
        zps = [p.ps(f"{tag}_zps{j}", [128, 512], F32, sc) for j in range(2)]
        zt = [p.sb(f"{tag}_zt{j}", [128, 512], F32, sc) for j in range(2)]
        zm = [p.sb(f"{tag}_zm{j}", [128, 512], F32, sc) for j in range(2)]
        zr = [p.sb(f"{tag}_zr{j}", [128, 512], F32, sc) for j in range(2)]
        zq = [p.sb(f"{tag}_zq{j}", [128, 512], F32, sc) for j in range(2)]
        one_col = p.sb(f"{tag}_one", [128, 1], F32, sc)
        p.op("gpsimd", lambda g: g.memset(one_col[:, :], 1.0), writes=[(tag, "onecol")])
        p.op("gpsimd", lambda g: g.dma_start(out=wgk2[:, :], in_=ws["wgk2"][:, :]),
             writes=[(tag, "wgk2")], dma="lpwgk")
        p.op("gpsimd", lambda g: g.dma_start(out=bgk2[:, :], in_=ws["bgk2"][:, :]),
             writes=[(tag, "bgk2")], dma="lpwgk")
        p.batch_sems.add("lpwgk")

        def cb_kv(p, ps, tb, nb, key):
            j = (tb * 3 + nb) % 6
            kb = (tag, "ob", j)
            p.op("vector", lambda v, ps=ps, j=j: v.tensor_copy(ob[j][:, :], ps[:, :]),
                 reads=[key], writes=[kb])
            if nb == 0:
                dst = hbm["k_tm"]; col = 0
            else:
                dst = hbm["v_tm"]; col = (nb - 1) * 512
            p.op("sync", lambda s, tb=tb, col=col, dst=dst, j=j: s.dma_start(
                out=dst[tb * 128:(tb + 1) * 128, col:col + 512], in_=ob[j][:, :]),
                reads=[kb], dma=f"lpst{j}")
        emit_proj_tm(p, ws["wkv"], 1536, xn_fm, T, cb_kv, tag + "kv", c, pp=1, xnkey=xnkey)

        def cb_ggk(p, ps, tb, nb, key):
            j = (tb * 3 + nb) % 6
            kb = (tag, "ob", j)
            if nb < 2:
                p.op("vector", lambda v, ps=ps, j=j: v.tensor_copy(ob[j][:, :], ps[:, :]),
                     reads=[key], writes=[kb])
                p.op("sync", lambda s, tb=tb, nb=nb, j=j: s.dma_start(
                    out=hbm["g_tm"][tb * 128:(tb + 1) * 128, nb * 512:(nb + 1) * 512],
                    in_=ob[j][:, :]),
                    reads=[kb], dma=f"lpst{j}")
            else:
                jj = tb % 2
                kg = (tag, "gkx", jj)
                p.op("vector", lambda v, ps=ps, jj=jj: v.tensor_copy(
                    gkx[jj][:, :], ps[:, 0:16]), reads=[key], writes=[kg])
                kgt = (tag, "gkxT", jj)
                p.op("tensor", lambda t, jj=jj: t.transpose(
                    gkxT[jj][:, :], gkx[jj][:, :], c["identb"][:, :]),
                    reads=[kg, "identb"], writes=[kgt])
                p.op("vector", lambda v, jj=jj: v.tensor_copy(
                    gkxTs[jj][:, :], gkxT[jj][:, :]), reads=[kgt], writes=[(tag, "gkxTs", jj)])
                p.op("tensor", lambda t, jj=jj: t.matmul(
                    zps[jj][:, :], gkxTs[jj][0:16, :], wgk2[0:16, :],
                    start=True, stop=False),
                    reads=[(tag, "gkxTs", jj), (tag, "wgk2")], war=[(tag, "zps", jj)], inc=False)
                p.op("tensor", lambda t, jj=jj: t.matmul(
                    zps[jj][:, :], c["onesb"][0:1, 0:128], bgk2[0:1, :],
                    start=False, stop=True),
                    reads=["onesb", (tag, "bgk2")], writes=[(tag, "zps", jj)])
                p.op("vector", lambda v, jj=jj: v.tensor_copy(
                    zm[jj][:, :], zps[jj][:, :]),
                    reads=[(tag, "zps", jj)], writes=[(tag, "zm", jj)])
                p.op("vector", lambda v, jj=jj: v.tensor_scalar_mul(
                    zr[jj][:, :], zm[jj][:, :], -1.0),
                    reads=[(tag, "zm", jj)], writes=[(tag, "zr", jj)])
                p.op("vector", lambda v, jj=jj: v.tensor_max(
                    zr[jj][:, :], zr[jj][:, :], zm[jj][:, :]),
                    reads=[(tag, "zr", jj), (tag, "zm", jj)], writes=[(tag, "zr", jj)])
                p.op("scalar", lambda s, jj=jj: s.activation(
                    zt[jj][:, :], zr[jj][:, :], AF.Exp, scale=-1.0),
                    reads=[(tag, "zr", jj)], writes=[(tag, "zt", jj)])
                p.op("scalar", lambda s, jj=jj: s.activation(
                    zt[jj][:, :], zt[jj][:, :], AF.Ln, bias=one_col[:, 0:1]),
                    reads=[(tag, "zt", jj), (tag, "onecol")], writes=[(tag, "zt", jj)])
                p.op("vector", lambda v, jj=jj: v.tensor_sub(
                    zq[jj][:, :], zm[jj][:, :], zr[jj][:, :]),
                    reads=[(tag, "zm", jj), (tag, "zr", jj)], writes=[(tag, "zq", jj)])
                p.op("vector", lambda v, jj=jj: v.tensor_scalar_mul(
                    zq[jj][:, :], zq[jj][:, :], 0.5),
                    reads=[(tag, "zq", jj)], writes=[(tag, "zq", jj)])
                p.op("vector", lambda v, jj=jj: v.tensor_sub(
                    zq[jj][:, :], zq[jj][:, :], zt[jj][:, :]),
                    reads=[(tag, "zq", jj), (tag, "zt", jj)], writes=[(tag, "zq", jj)])
                p.op("sync", lambda s, tb=tb, jj=jj: s.dma_start(
                    out=hbm["gk_tm"][tb * 128:(tb + 1) * 128, :], in_=zq[jj][:, :]),
                    reads=[(tag, "zq", jj)], dma=f"lpgk{jj}")
        emit_proj_tm(p, ws["wggk"], 1536, xn_fm, T, cb_ggk, tag + "gg", c, pp=1, xnkey=xnkey)
        p.barrier()


def emit_l0_pass1(p, hbm, T, s_out, c, lt64_dram, dbg=None, tag="p1"):
    NCH = T // 64
    with ExitStack() as sc:
        lt = p.sb(f"{tag}_lt", [64, 64], F32, sc)
        p.op("sync", lambda s: s.dma_start(out=lt[:, :], in_=lt64_dram[:, :]),
             writes=[(tag, "lt64")], dma=f"{tag}c")
        onecolf = p.sb(f"{tag}_onecolf", [64, 1], F32, sc)
        p.op("gpsimd", lambda g: g.memset(onecolf[:, :], 1.0), writes=[(tag, "onecolf")])
        S = p.sb(f"{tag}_S", [64, 1024], F32, sc)
        p.op("gpsimd", lambda g: g.memset(S[:, :], 0.0), writes=[(tag, "S")])
        kt = [p.sb(f"{tag}_kt{j}", [64, 512], BF16, sc) for j in range(2)]
        vt = [p.sb(f"{tag}_vt{j}", [64, 1024], BF16, sc) for j in range(2)]
        gk = [p.sb(f"{tag}_gk{j}", [64, 512], F32, sc) for j in range(2)]
        bps = [p.ps(f"{tag}_b{j}", [64, 512], F32, sc) for j in range(2)]
        eiv = [p.sb(f"{tag}_ei{j}", [64, 512], BF16, sc) for j in range(2)]
        ket = [p.sb(f"{tag}_ke{j}", [64, 512], BF16, sc) for j in range(2)]
        elp = [p.ps(f"{tag}_el{j}", [64, 16], F32, sc) for j in range(2)]
        els = [p.sb(f"{tag}_els{j}", [64, 8], F32, sc) for j in range(2)]
        cps = p.ps(f"{tag}_c", [64, 1024], F32, sc)
        dbg_b = p.sb("dbg_b", [64, 512], F32, sc) if dbg is not None else None
        for n in range(NCH):
            j = n % 2
            kk, kv, kg = (tag, "k", j), (tag, "v", j), (tag, "g", j)
            kb, kei, kke, kel = (tag, "b", j), (tag, "ei", j), (tag, "ke", j), (tag, "el", j)
            p.op("sync", lambda s, n=n, j=j: s.dma_start(
                out=kt[j][:, :], in_=hbm["k_tm"][n * 64:(n + 1) * 64, :]),
                writes=[kk], dma=f"{tag}ld{j}")
            p.op("sync", lambda s, n=n, j=j: s.dma_start(
                out=vt[j][:, :], in_=hbm["v_tm"][n * 64:(n + 1) * 64, :]),
                writes=[kv], dma=f"{tag}ld{j}")
            p.op("sync", lambda s, n=n, j=j: s.dma_start(
                out=gk[j][:, :], in_=hbm["gk_tm"][n * 64:(n + 1) * 64, :]),
                writes=[kg], dma=f"{tag}ld{j}")
            p.op("tensor", lambda t, j=j: t.matmul(
                bps[j][:, :], lt[:, :], gk[j][:, :], start=True, stop=True),
                reads=[(tag, "lt64"), kg], war=[kb], writes=[kb])
            p.op("scalar", lambda s, j=j: s.activation(
                eiv[j][:, :], bps[j][:, :], AF.Exp, scale=-1.0),
                reads=[kb], writes=[kei])
            p.op("vector", lambda v, j=j: v.tensor_mul(
                ket[j][:, :], kt[j][:, :], eiv[j][:, :]), reads=[kk, kei], writes=[kke])
            for h in range(8):
                p.op("tensor", lambda t, j=j, h=h: t.matmul(
                    elp[j][:, 2 * h:2 * h + 1], gk[j][:, h * 64:(h + 1) * 64],
                    onecolf[:, 0:1],
                    start=True, stop=True),
                    reads=[kg, (tag, "onecolf")], war=[kel] if h == 0 else (),
                    writes=[kel] if h == 7 else (), inc=(h == 7))
            p.op("scalar", lambda s, j=j: s.activation(
                els[j][:, :], bass.AP(elp[j], 0, [[16, 64], [2, 8]]),
                AF.Exp, scale=1.0 / 16.0),
                reads=[kel], writes=[(tag, "els", j)])
            for h in range(8):
                p.op("tensor", lambda t, j=j, h=h: t.matmul(
                    cps[:, h * 128:(h + 1) * 128], ket[j][:, h * 64:(h + 1) * 64],
                    vt[j][:, h * 128:(h + 1) * 128], start=True, stop=True),
                    reads=[kke, kv], war=[(tag, "c")] if h == 0 else (),
                    writes=[(tag, "c")] if h == 7 else (), inc=(h == 7))
            p.op("vector", lambda v: v.tensor_add(S[:, :], S[:, :], cps[:, :]),
                 reads=[(tag, "c"), (tag, "S")], writes=[(tag, "S")])
            p.op("vector", lambda v, j=j: v.tensor_mul(
                S[:, :], S[:, :],
                bass.AP(els[j], 0, [[8, 64], [1, 8], [0, 128]])),
                reads=[(tag, "S"), (tag, "els", j)], writes=[(tag, "S")])
            if dbg is not None and n == 0:
                for nm, tile in [("b", bps[j]), ("ei", eiv[j]), ("ke", ket[j]),
                                 ("els", els[j]), ("S0", S)]:
                    if nm in dbg:
                        if nm == "b":
                            p.op("vector", lambda v, tile=tile: v.tensor_copy(
                                dbg_b[:, :], tile[:, :]), reads=[(tag, "b", j)],
                                writes=["dbgb"])
                            tile = dbg_b
                        p.op("sync", lambda s_, nm=nm, tile=tile: s_.dma_start(
                            out=dbg[nm][:, :], in_=tile[:, :]),
                            reads=[(tag, "S"), "dbgb", (tag, "ei", j), (tag, "ke", j),
                                   (tag, "els", j)],
                            dma=f"ddbg{nm}")
        p.op("sync", lambda s: s.dma_start(out=s_out[:, :], in_=S[:, :]),
             reads=[(tag, "S")], dma=f"{tag}c")
        p.barrier()



def make_qk_bufs(p, sc, tagp="qkb"):
    return {"wsb": p.sb(f"{tagp}_w", [128, 8 * 512], BF16, sc),
            "oq": [p.sb(f"{tagp}_o{j}", [64, 512], BF16, sc) for j in range(2)]}


def emit_l0_qk_fm(p, wq_hbm, wk_hbm, xn_fm, T, q_hbm, k_hbm, tagp="qk",
                  xnkey="xnfm", bufs=None):
    with ExitStack() as sc0:
        if bufs is None:
            bufs = make_qk_bufs(p, sc0, tagp + "b")
        for nm, w_hbm, out in (("q", wq_hbm, q_hbm), ("k", wk_hbm, k_hbm)):
            sc = ExitStack()
            sc.__enter__()
            tg = "qkv_" + nm
            wsb = bufs["wsb"]
            pq = [p.ps(f"{tagp}{nm}_p{j}", [64, 512], F32, sc) for j in range(2)]
            p.batch_sems.add("qkw")
            for k in range(8):
                p.op("gpsimd", lambda g, k=k, wsb=wsb, w_hbm=w_hbm: g.dma_start(
                    out=wsb[:, k * 512:(k + 1) * 512],
                    in_=w_hbm[k * 128:(k + 1) * 128, :]),
                    writes=[("qkvw",)] if k == 7 else (),
                    war=[("qkvw",)] if k == 0 else (), dma="qkw")
            oq = bufs["oq"]
            W = min(512, T)
            for h in range(8):
                for tb in range(T // W):
                    j = tb % 2
                    key = (tg, "p", j)
                    for k in range(8):
                        p.op("tensor", lambda t, h=h, tb=tb, k=k, j=j, wsb=wsb,
                             pq=pq, W=W: t.matmul(
                            pq[j][:, 0:W],
                            wsb[:, k * 512 + h * 64: k * 512 + (h + 1) * 64],
                            xn_fm[:, k * T + tb * W: k * T + (tb + 1) * W],
                            start=(k == 0), stop=(k == 7)),
                            reads=[("qkvw",), (xnkey, k)],
                            war=[key] if k == 0 else (),
                            writes=[key] if k == 7 else (), inc=(k == 7))
                    p.op("vector", lambda v, j=j, oq=oq, pq=pq, W=W:
                         v.tensor_copy(oq[j][:, 0:W], pq[j][:, 0:W]),
                        reads=[key], writes=[(tg, "o", j)])
                    p.op("sync", lambda sy, h=h, tb=tb, j=j, oq=oq, W=W, out=out:
                         sy.dma_start(
                        out=out[h * 64:(h + 1) * 64, tb * W:(tb + 1) * W],
                        in_=oq[j][:, 0:W]),
                        reads=[(tg, "o", j)], dma=f"qko{j}")
            p.barrier()
            sc.__exit__(None, None, None)


def emit_l0_pass2(p, hbm, T, q_hbm, k_hbm, s_in, wo_hbm, nw8_dram, mask8_dram,
                  lt64_dram, attn_fm_hbm, c, flag_dram=None):
    NCH = T // 64
    TTW = min(512, T)
    TCH = TTW // 64
    with ExitStack() as sc:
        lt = p.sb("p2_lt", [64, 64], F32, sc)
        p.op("sync", lambda s: s.dma_start(out=lt[:, :], in_=lt64_dram[:, :]),
             writes=["lt64"], dma="p2c")
        mask8 = p.sb("p2_mask", [64, 512], BF16, sc)
        p.op("sync", lambda s: s.dma_start(out=mask8[:, :], in_=mask8_dram[:, :]),
             writes=["mask8"], dma="p2c")
        S = p.sb("p2_S", [64, 1024], F32, sc)
        p.op("sync", lambda s: s.dma_start(out=S[:, :], in_=s_in[:, :]),
             writes=["S"], dma="p2c")
        if flag_dram is not None:
            flg = p.sb("p2_flag", [64, 1], F32, sc)
            p.op("sync", lambda s: s.dma_start(out=flg[:, :], in_=flag_dram[:, :]),
                 writes=["flag"], dma="p2c")
            p.op("vector", lambda v: v.tensor_scalar_mul(S[:, :], S[:, :],
                                                         flg[:, 0:1]),
                 reads=["S", "flag"], writes=["S"])
        Sb = p.sb("p2_Sb", [64, 1024], BF16, sc)
        wo = p.sb("p2_wo", [128, 8 * 1024], BF16, sc)
        p.batch_sems.add("p2cg")
        for k in range(8):
            p.op("gpsimd", lambda g, k=k: g.dma_start(
                out=wo[:, k * 1024:(k + 1) * 1024],
                in_=wo_hbm[k * 128:(k + 1) * 128, :]), writes=["wo"], dma="p2cg")
        nw8r = p.sb("p2_nw8r", [1, 1024], BF16, sc)
        p.op("gpsimd", lambda g: g.dma_start(out=nw8r[:, :], in_=nw8_dram[:, :]),
             writes=["nw8r"], dma="p2cg")
        nwt = p.sb("p2_nwt", [64, 1024], BF16, sc)
        with ExitStack() as sc2:
            pnw = p.ps("p2_pnw", [64, 1024], F32, sc2)
            for hf in range(2):
                p.op("tensor", lambda t, hf=hf: t.matmul(
                    pnw[:, hf * 512:(hf + 1) * 512], c["onesb"][0:1, 0:64],
                    nw8r[0:1, hf * 512:(hf + 1) * 512], start=True, stop=True),
                    reads=["nw8r", "onesb"], writes=[("pnw",)] if hf else ())
            p.op("vector", lambda v: v.tensor_copy(nwt[:, :], pnw[:, :]),
                 reads=[("pnw",)], writes=["nwt"])
            p.barrier()

        kt = [p.sb(f"p2_kt{j}", [64, 512], BF16, sc) for j in range(2)]
        vt = [p.sb(f"p2_vt{j}", [64, 1024], BF16, sc) for j in range(2)]
        gk = [p.sb(f"p2_gk{j}", [64, 512], F32, sc) for j in range(2)]
        gt = [p.sb(f"p2_gt{j}", [64, 1024], BF16, sc) for j in range(2)]
        bps = p.ps("p2_b", [64, 512], F32, sc)
        et = p.sb("p2_et", [64, 512], BF16, sc)
        eiv = p.sb("p2_ei", [64, 512], BF16, sc)
        pefm = p.ps("p2_pefm", [64, 1024], BF16, sc)
        qe = p.sb("p2_qe", [64, 512], BF16, sc)
        kef = p.sb("p2_kef", [64, 512], BF16, sc)
        ket = p.sb("p2_ket", [64, 512], BF16, sc)
        pA = p.ps("p2_pA", [64, 512], F32, sc)
        am = p.sb("p2_am", [64, 512], BF16, sc)
        pO = p.ps("p2_pO", [64, 1024], F32, sc)
        pC = p.ps("p2_pC", [64, 512], F32, sc)
        osq = p.sb("p2_osq", [64, 1024], F32, sc)
        rs = p.sb("p2_rs", [64, 8], F32, sc)
        on = p.sb("p2_on", [64, 1024], F32, sc)
        og = [p.sb(f"p2_og{j}", [64, 1024], BF16, sc) for j in range(2)]
        pT = p.ps("p2_pT", [128, 64], BF16, sc)
        ogfm = p.sb("p2_ogfm", [128, 8 * TTW], BF16, sc)
        pWO = [p.ps("p2_pWO0", [128, 512], F32, sc)]
        ow = [p.sb(f"p2_ow{j}", [128, 512], F32, sc) for j in range(2)]

        qcf = [p.sb(f"p2_qc{j}", [64, 512], BF16, sc) for j in range(2)]
        kcf = [p.sb(f"p2_kc{j}", [64, 512], BF16, sc) for j in range(2)]

        def qview(src, n):
            return src[:, :].rearrange("p (h t) -> p h t", h=8)

        for n in range(NCH):
            j = n % 2
            tt = n // TCH
            cc = n % TCH
            kk, kv, kg, kgt = ("k", j), ("v", j), ("g", j), ("gt", j)
            p.op("sync", lambda s, n=n, j=j: s.dma_start(
                out=kt[j][:, :], in_=hbm["k_tm"][n * 64:(n + 1) * 64, :]),
                writes=[kk], dma=f"p2ld{j}")
            p.op("sync", lambda s, n=n, j=j: s.dma_start(
                out=vt[j][:, :], in_=hbm["v_tm"][n * 64:(n + 1) * 64, :]),
                writes=[kv], dma=f"p2ld{j}")
            p.op("sync", lambda s, n=n, j=j: s.dma_start(
                out=gk[j][:, :], in_=hbm["gk_tm"][n * 64:(n + 1) * 64, :]),
                writes=[kg], dma=f"p2ld{j}")
            p.op("sync", lambda s, n=n, j=j: s.dma_start(
                out=gt[j][:, :], in_=hbm["g_tm"][n * 64:(n + 1) * 64, :]),
                writes=[kgt], dma=f"p2ld{j}")
            p.op("sync", lambda s, n=n, j=j: s.dma_start(
                out=qcf[j][:, :].rearrange("p (h t) -> p h t", h=8),
                in_=q_hbm[:, n * 64:(n + 1) * 64].rearrange(
                    "(h p) t -> p h t", p=64)),
                writes=[("p2q", j)], dma=f"p2ld{j}")
            p.op("sync", lambda s, n=n, j=j: s.dma_start(
                out=kcf[j][:, :].rearrange("p (h t) -> p h t", h=8),
                in_=k_hbm[:, n * 64:(n + 1) * 64].rearrange(
                    "(h p) t -> p h t", p=64)),
                writes=[("p2k", j)], dma=f"p2ld{j}")
            p.op("tensor", lambda t, j=j: t.matmul(
                bps[:, :], lt[:, :], gk[j][:, :], start=True, stop=True),
                reads=["lt64", kg], war=[("b",)], writes=[("b",)])
            p.op("scalar", lambda s: s.activation(et[:, :], bps[:, :], AF.Exp),
                 reads=[("b",)], writes=[("et",)])
            p.op("scalar", lambda s: s.activation(eiv[:, :], bps[:, :], AF.Exp,
                                                  scale=-1.0),
                 reads=[("b",)], writes=[("ei",)])
            for h in range(8):
                p.op("tensor", lambda t, h=h: t.transpose(
                    pefm[:, h * 64:(h + 1) * 64], et[:, h * 64:(h + 1) * 64],
                    c["identb"][0:64, 0:64]),
                    reads=[("et",), "identb"], war=[("ef",)] if h == 0 else (),
                    inc=False)
            for h in range(8):
                p.op("tensor", lambda t, h=h: t.transpose(
                    pefm[:, 512 + h * 64: 512 + (h + 1) * 64],
                    eiv[:, h * 64:(h + 1) * 64],
                    c["identb"][0:64, 0:64]),
                    reads=[("ei",), "identb"],
                    writes=[("ef",)] if h == 7 else (), inc=(h == 7))
            p.op("vector", lambda v, j=j: v.scalar_tensor_tensor(
                qe[:, :], qview(qcf[j], n), 0.125, pefm[:, 0:512].rearrange(
                    "p (h t) -> p h t", h=8), ALU.mult, ALU.mult),
                reads=[("ef",), ("p2q", j)], writes=[("qe",)])
            p.op("vector", lambda v, j=j: v.tensor_mul(
                kef[:, :], qview(kcf[j], n), pefm[:, 512:1024].rearrange(
                    "p (h t) -> p h t", h=8)),
                reads=[("ef",), ("p2k", j)], writes=[("kef",)])
            p.op("vector", lambda v, j=j: v.tensor_mul(
                ket[:, :], kt[j][:, :], eiv[:, :]), reads=[kk, ("ei",)],
                writes=[("ket",)])
            p.op("vector", lambda v: v.tensor_copy(Sb[:, :], S[:, :]),
                 reads=["S"], writes=[("Sb",)])
            for h in range(8):
                p.op("tensor", lambda t, h=h: t.matmul(
                    pA[:, h * 64:(h + 1) * 64], kef[:, h * 64:(h + 1) * 64],
                    qe[:, h * 64:(h + 1) * 64], start=True, stop=True),
                    reads=[("kef",), ("qe",)], war=[("pA",)] if h == 0 else (),
                    writes=[("pA",)] if h == 7 else (), inc=(h == 7))
            p.op("vector", lambda v: v.tensor_mul(am[:, :], pA[:, :], mask8[:, :]),
                 reads=[("pA",), "mask8"], writes=[("am",)])
            for h in range(8):
                p.op("tensor", lambda t, h=h, j=j: t.matmul(
                    pO[:, h * 128:(h + 1) * 128], am[:, h * 64:(h + 1) * 64],
                    vt[j][:, h * 128:(h + 1) * 128], start=True, stop=False),
                    reads=[("am",), kv], war=[("pO",)] if h == 0 else (), inc=False)
                p.op("tensor", lambda t, h=h: t.matmul(
                    pO[:, h * 128:(h + 1) * 128], qe[:, h * 64:(h + 1) * 64],
                    Sb[:, h * 128:(h + 1) * 128], start=False, stop=True),
                    reads=[("qe",), ("Sb",)],
                    writes=[("pO",)] if h == 7 else (), inc=(h == 7))
            for half in range(2):
                for hh in range(4):
                    h = half * 4 + hh
                    p.op("tensor", lambda t, h=h, hh=hh, j=j: t.matmul(
                        pC[:, hh * 128:(hh + 1) * 128], ket[:, h * 64:(h + 1) * 64],
                        vt[j][:, h * 128:(h + 1) * 128], start=True, stop=True),
                        reads=[("ket",), kv], war=[("pC",)] if hh == 0 else (),
                        writes=[("pC",)] if hh == 3 else (), inc=(hh == 3))
                p.op("vector", lambda v, half=half: v.tensor_add(
                    S[:, half * 512:(half + 1) * 512],
                    S[:, half * 512:(half + 1) * 512], pC[:, :]),
                    reads=[("pC",), "S", ("Sb",)], writes=["S"])
            p.op("vector", lambda v: v.tensor_mul(
                S[:, :].rearrange("p (h v) -> p h v", h=8),
                S[:, :].rearrange("p (h v) -> p h v", h=8),
                bass.AP(pefm, 63, [[1024, 64], [64, 8], [0, 128]])),
                reads=["S", ("ef",)], writes=["S"])
            p.op("scalar", lambda s: s.activation(osq[:, :], pO[:, :], AF.Square),
                 reads=[("pO",)], writes=[("osq",)])
            p.op("vector", lambda v: v.reduce_sum(
                rs[:, :], osq[:, :].rearrange("p (h v) -> p h v", h=8), axis=AXX),
                reads=[("osq",)], writes=[("rs",)])
            p.op("scalar", lambda s: s.activation(rs[:, :], rs[:, :], AF.Ln,
                                                  bias=c["eps"][0:64, 0:1],
                                                  scale=1.0 / 128),
                 reads=[("rs",), "eps"], writes=[("rs",)])
            p.op("scalar", lambda s: s.activation(rs[:, :], rs[:, :], AF.Exp,
                                                  scale=-0.5),
                 reads=[("rs",)], writes=[("rs",)])
            p.op("vector", lambda v: v.tensor_mul(
                on[:, :].rearrange("p (h v) -> p h v", h=8),
                pO[:, :].rearrange("p (h v) -> p h v", h=8),
                bass.AP(rs, 0, [[8, 64], [1, 8], [0, 128]])),
                reads=[("pO",), ("rs",)], writes=[("on",)])
            p.op("scalar", lambda s, j=j: s.activation(
                gt[j][:, :], gt[j][:, :], AF.Silu), reads=[kgt], writes=[kgt])
            p.op("vector", lambda v: v.tensor_mul(on[:, :], on[:, :], nwt[:, :]),
                 reads=[("on",), "nwt"], writes=[("on",)])
            p.op("vector", lambda v, j=j: v.tensor_mul(
                og[j][:, :], on[:, :], gt[j][:, :]),
                reads=[("on",), kgt], writes=[("og", j)])
            for vb in range(8):
                kpt = ("pT",)
                p.op("tensor", lambda t, vb=vb, j=j: t.transpose(
                    pT[:, :], og[j][:, vb * 128:(vb + 1) * 128],
                    c["identb"][0:64, 0:64]),
                    reads=[("og", j), "identb"], war=[kpt], writes=[kpt])
                first = (cc == 0 and vb == 0)
                p.op("vector", lambda v, vb=vb, cc=cc: v.tensor_copy(
                    ogfm[:, vb * TTW + cc * 64: vb * TTW + (cc + 1) * 64],
                    pT[:, :]),
                    reads=[kpt],
                    writes=[("ogfm",)] if first else (),
                    wlast=[("ogfm",)] if not first else ())
            if cc == TCH - 1:
                for dblk in range(8):
                    jj = (tt * 8 + dblk) % 2
                    jw = 0
                    keyw = ("pWO",)
                    for k in range(8):
                        p.op("tensor", lambda t, dblk=dblk, k=k, jw=jw: t.matmul(
                            pWO[jw][:, 0:TTW], wo[:, k * 1024 + dblk * 128:
                                              k * 1024 + (dblk + 1) * 128],
                            ogfm[:, k * TTW:(k + 1) * TTW],
                            start=(k == 0), stop=(k == 7)),
                            reads=["wo", ("ogfm",)],
                            war=[keyw] if k == 0 else (),
                            writes=[keyw] if k == 7 else (), inc=(k == 7))
                    p.op("vector", lambda v, jj=jj: v.tensor_copy(
                        ow[jj][:, 0:TTW], pWO[0][:, 0:TTW]), reads=[keyw],
                        writes=[("ow", jj)])
                    p.op("sync", lambda s, tt=tt, dblk=dblk, jj=jj: s.dma_start(
                        out=attn_fm_hbm[dblk * 128:(dblk + 1) * 128,
                                        tt * TTW:(tt + 1) * TTW],
                        in_=ow[jj][:, 0:TTW]),
                        reads=[("ow", jj)], dma=f"dattn{jj}")
        p.barrier()


def emit_l0_router(p, rwt_dram, xn_fm, T, comb_dram, c, tagp="rt", xnkey="xnfm"):
    with ExitStack() as sc:
        lg = p.sb(f"{tagp}_lg", [128, 8], F32, sc)
        st = p.sb(f"{tagp}_st", [128, 4], F32, sc)
        e = p.sb(f"{tagp}_e", [128, 8], F32, sc)
        ge2 = p.sb(f"{tagp}_g2", [128, 8], F32, sc)
        cmb = p.sb(f"{tagp}_cmb", [128, 8], BF16, sc)
        pcT = p.ps(f"{tagp}_pcT", [8, 128], BF16, sc)
        cTs = [p.sb(f"{tagp}_cTs{j}", [8, 128], BF16, sc) for j in range(2)]

        def cb(p, ps, tb, nb, key):
            kl, ke = (tagp, "lg"), (tagp, "ebuf")
            p.op("vector", lambda v, ps=ps: v.tensor_copy(lg[:, :], ps[:, 0:8]),
                 reads=[key], writes=[kl])
            p.op("vector", lambda v: v.reduce_max(st[:, 0:1], lg[:, :], axis=AXX),
                 reads=[kl], writes=[(tagp, "m1")])
            p.op("vector", lambda v: v.tensor_scalar_mul(st[:, 2:3], st[:, 0:1], -1.0),
                 reads=[(tagp, "m1")], writes=[(tagp, "nm1")])
            p.op("vector", lambda v: v.tensor_scalar(
                e[:, :], lg[:, :], st[:, 0:1], None, ALU.is_ge),
                reads=[kl, (tagp, "m1")], writes=[ke])
            p.op("vector", lambda v: v.scalar_tensor_tensor(
                e[:, :], e[:, :], -1e30, lg[:, :], ALU.mult, ALU.add),
                reads=[ke, kl], writes=[ke])
            p.op("vector", lambda v: v.reduce_max(st[:, 1:2], e[:, :], axis=AXX),
                 reads=[ke], writes=[(tagp, "m2")])
            p.op("vector", lambda v: v.tensor_scalar(
                ge2[:, :], lg[:, :], st[:, 1:2], None, ALU.is_ge),
                reads=[kl, (tagp, "m2")], writes=[(tagp, "ge2")])
            p.op("scalar", lambda s: s.activation(
                e[:, :], lg[:, :], AF.Exp, bias=st[:, 2:3]),
                reads=[kl, (tagp, "nm1")], writes=[ke])
            p.op("scalar", lambda s: s.activation(
                st[:, 3:4], st[:, 1:2], AF.Exp, bias=st[:, 2:3]),
                reads=[(tagp, "m2"), (tagp, "nm1")], writes=[(tagp, "dn")])
            p.op("vector", lambda v: v.tensor_scalar_add(st[:, 3:4], st[:, 3:4], 1.0),
                 reads=[(tagp, "dn")], writes=[(tagp, "dn")])
            p.op("vector", lambda v: v.reciprocal(st[:, 3:4], st[:, 3:4]),
                 reads=[(tagp, "dn")], writes=[(tagp, "dn")])
            p.op("vector", lambda v: v.scalar_tensor_tensor(
                cmb[:, :], e[:, :], st[:, 3:4], ge2[:, :], ALU.mult, ALU.mult),
                reads=[ke, (tagp, "dn"), (tagp, "ge2")],
                writes=[(tagp, "cmb")])
            p.op("tensor", lambda t: t.transpose(
                pcT[:, :], cmb[:, :], c["identb"][:, :]),
                reads=[(tagp, "cmb"), "identb"], war=[(tagp, "pcT")],
                writes=[(tagp, "pcT")])
            jj = tb % 2
            p.op("vector", lambda v, jj=jj: v.tensor_copy(
                cTs[jj][:, :], pcT[:, :]),
                reads=[(tagp, "pcT")], writes=[(tagp, "cTs", jj)])
            p.op("sync", lambda sy, tb=tb, jj=jj: sy.dma_start(
                out=comb_dram[:, tb * 128:(tb + 1) * 128], in_=cTs[jj][:, :]),
                reads=[(tagp, "cTs", jj)], dma=f"rtdc{jj}")
        emit_proj_tm(p, rwt_dram, 512, xn_fm, T, cb, tagp + "p", c, pp=1)
        p.barrier()


def make_ffn_tiles(p, sc, WB, NM=32, tag="ffn"):
    ft = {}
    ft["WB"] = WB
    ft["wg"] = [p.sb(f"{tag}_wg{j}", [128, 8 * 128], BF16, sc) for j in range(2)]
    ft["wu"] = [p.sb(f"{tag}_wu{j}", [128, 8 * 128], BF16, sc) for j in range(2)]
    ft["pg"] = [p.ps(f"{tag}_pg{j}", [128, WB], F32, sc) for j in range(2)]
    ft["pu"] = [p.ps(f"{tag}_pu{j}", [128, WB], F32, sc) for j in range(2)]
    ft["sg"] = [p.sb(f"{tag}_sg{j}", [128, WB], F32, sc) for j in range(2)]
    ft["wd"] = [p.sb(f"{tag}_wd{j}", [128, NM * 128], BF16, sc) for j in range(2)]
    ft["py"] = [p.ps(f"{tag}_py{j}", [128, WB], F32, sc) for j in range(2)]
    ft["cbs"] = [p.sb(f"{tag}_cb{j}", [128, WB], BF16, sc) for j in range(2)]
    ft["yt"] = [p.sb(f"{tag}_yt{j}", [128, WB], F32, sc) for j in range(2)]
    ft["pcb"] = p.ps(f"{tag}_pcb", [128, WB], F32, sc)
    ft["crow"] = p.sb(f"{tag}_crow", [1, WB], BF16, sc)
    ft["b1"] = p.sb(f"{tag}_b1", [128, NM], F32, sc)
    ft["b2"] = p.sb(f"{tag}_b2", [1, 1024], BF16, sc)
    return ft


def emit_ffn_expert(p, tagp, ft, xn_fm, T, tt, TT, w1_dram, w2_dram, hs_fm, yacc,
                    c, w1g_dram=None, b1_dram=None, b2_dram=None, comb_dram=None,
                    e_idx=0, first=False, F=4096, act=AF.Silu, xnkey="xnfm"):
    NM = F // 128
    NTB = max(1, TT // 512)
    WB = ft["WB"]
    assert WB == min(512, TT)
    wg, wu, pg, pu, sg = ft["wg"], ft["wu"], ft["pg"], ft["pu"], ft["sg"]
    wd, py, cbs, yt = ft["wd"], ft["py"], ft["cbs"], ft["yt"]
    pcb, crow, b1c, b2r = ft["pcb"], ft["crow"], ft["b1"], ft["b2"]
    if b1_dram is not None:
        p.op("sync", lambda g: g.dma_start(out=b1c[:, 0:NM], in_=b1_dram[:, :]),
             writes=[("fb1",)], dma="ffndbs")
    if b2_dram is not None:
        p.op("gpsimd", lambda g: g.dma_start(out=b2r[:, :], in_=b2_dram[:, :]),
             writes=[("fb2",)], dma="ffndb")
    for m in range(NM):
        jm = m % 2
        kwu, kwg = ("fwu", jm), ("fwg", jm)
        p.op("gpsimd", lambda g, m=m, jm=jm, w1_dram=w1_dram: g.dma_start(
            out=wu[jm][:, :],
            in_=w1_dram[m].rearrange("p k f -> p (k f)")),
            writes=[kwu], dma=f"ffnw{jm}")
        if w1g_dram is not None:
            p.op("gpsimd", lambda g, m=m, jm=jm, w1g_dram=w1g_dram: g.dma_start(
                out=wg[jm][:, :],
                in_=w1g_dram[m].rearrange("p k f -> p (k f)")),
                writes=[kwg], dma=f"ffnw{jm}")
        for tb in range(NTB):
            jb = tb % 2
            kg, ku = ("fpg", jb), ("fpu", jb)
            for k in range(8):
                p.op("tensor", lambda t, m=m, tb=tb, k=k, jb=jb, jm=jm: t.matmul(
                    pu[jb][:, :], wu[jm][:, k * 128:(k + 1) * 128],
                    xn_fm[:, k * T + tt * TT + tb * WB:
                          k * T + tt * TT + tb * WB + WB],
                    start=(k == 0), stop=(k == 7)),
                    reads=[kwu, (xnkey, k)], war=[ku] if k == 0 else (),
                    writes=[ku] if k == 7 else (), inc=(k == 7))
            if w1g_dram is not None:
                for k in range(8):
                    p.op("tensor", lambda t, m=m, tb=tb, k=k, jb=jb, jm=jm:
                         t.matmul(
                        pg[jb][:, :], wg[jm][:, k * 128:(k + 1) * 128],
                        xn_fm[:, k * T + tt * TT + tb * WB:
                              k * T + tt * TT + tb * WB + WB],
                        start=(k == 0), stop=(k == 7)),
                        reads=[kwg, (xnkey, k)], war=[kg] if k == 0 else (),
                        writes=[kg] if k == 7 else (), inc=(k == 7))
                p.op("scalar", lambda s, jb=jb: s.activation(
                    sg[jb][:, :], pg[jb][:, :], act),
                    reads=[kg], writes=[("fsg", jb)])
                p.op("vector", lambda v, m=m, tb=tb, jb=jb: v.tensor_mul(
                    hs_fm[:, m * TT + tb * WB: m * TT + tb * WB + WB],
                    sg[jb][:, :], pu[jb][:, :]),
                    reads=[("fsg", jb), ku],
                    writes=[("hs",)] if (m == 0 and tb == 0) else (),
                    wlast=[("hs",)] if not (m == 0 and tb == 0) else ())
            else:
                p.op("scalar", lambda s, m=m, tb=tb, jb=jb: s.activation(
                    hs_fm[:, m * TT + tb * WB: m * TT + tb * WB + WB],
                    pu[jb][:, :], act, bias=b1c[:, m:m + 1]),
                    reads=[ku, ("fb1",)],
                    writes=[("hs",)] if (m == 0 and tb == 0) else (),
                    wlast=[("hs",)] if not (m == 0 and tb == 0) else ())
    if comb_dram is not None:
        for tb in range(NTB):
            p.op("sync", lambda sy, tb=tb, comb_dram=comb_dram: sy.dma_start(
                out=crow[0:1, :],
                in_=comb_dram[e_idx:e_idx + 1,
                              tt * TT + tb * WB: tt * TT + tb * WB + WB]),
                writes=[("fcrow",)], dma="ffndcr")
            p.op("tensor", lambda t, tb=tb: t.matmul(
                pcb[:, :], c["onesb"][0:1, 0:128], crow[0:1, :],
                start=True, stop=True),
                reads=[("fcrow",), "onesb"], war=[("fpcb",)],
                writes=[("fpcb",)])
            p.op("vector", lambda v, tb=tb: v.tensor_copy(
                cbs[tb % 2][:, :], pcb[:, :]),
                reads=[("fpcb",)], writes=[("fcbs", tb % 2)])
    for dblk in range(8):
        jd = dblk % 2
        kwd = ("fwd", jd)
        p.op("gpsimd", lambda g, dblk=dblk, jd=jd, NM=NM, w2_dram=w2_dram:
             g.dma_start(
            out=wd[jd][:, 0:NM * 128],
            in_=w2_dram[dblk].rearrange("p k f -> p (k f)")),
            writes=[kwd], dma=f"ffndwd{jd}")
        for tb in range(NTB):
            jb = tb % 2
            ky = ("fpy", jb)
            for k2 in range(NM):
                last = (k2 == NM - 1 and b2_dram is None)
                p.op("tensor", lambda t, dblk=dblk, tb=tb, k2=k2, jb=jb,
                     jd=jd, last=last: t.matmul(
                    py[jb][:, :], wd[jd][:, k2 * 128:(k2 + 1) * 128],
                    hs_fm[:, k2 * TT + tb * WB: k2 * TT + tb * WB + WB],
                    start=(k2 == 0), stop=last),
                    reads=[kwd, ("hs",)], war=[ky] if k2 == 0 else (),
                    writes=[ky] if last else (), inc=last)
            if b2_dram is not None:
                p.op("tensor", lambda t, dblk=dblk, jb=jb: t.matmul(
                    py[jb][:, :], b2r[0:1, dblk * 128:(dblk + 1) * 128],
                    c["onesb"][0:1, 0:WB],
                    start=False, stop=True),
                    reads=[("fb2",), "onesb"], writes=[ky])
            dst = yacc[:, dblk * TT + tb * WB: dblk * TT + tb * WB + WB]
            if comb_dram is not None:
                p.op("vector", lambda v, jb=jb, tb=tb: v.tensor_mul(
                    yt[jb][:, :], py[jb][:, :], cbs[tb % 2][:, :]),
                    reads=[ky, ("fcbs", tb % 2)], writes=[("fyt", jb)])
            else:
                p.op("vector", lambda v, jb=jb: v.tensor_copy(
                    yt[jb][:, :], py[jb][:, :]),
                    reads=[ky], writes=[("fyt", jb)])
            if first:
                p.op("vector", lambda v, jb=jb, dst=dst: v.tensor_copy(
                    dst, yt[jb][:, :]),
                    reads=[("fyt", jb)], writes=[("yacc",)])
            else:
                p.op("vector", lambda v, jb=jb, dst=dst: v.tensor_add(
                    dst, dst, yt[jb][:, :]),
                    reads=[("fyt", jb), ("yacc",)], wlast=[("yacc",)])


def emit_combine_x1(p, yacc, tt, TT, T, x_hbm, attn_fm_hbm, out_hbm, c, pTs,
                    tagp="cmb"):
    p.barrier()
    blocks = []
    for t_ in pTs:
        w = t_.shape[1]
        for co in range(0, (w // 128) * 128, 128):
            if len(blocks) < 8:
                blocks.append((t_, co))
    assert len(blocks) == 8
    with ExitStack() as sc:
        af = p.sb(f"{tagp}_af", [128, 128], F32, sc)
        xt = [p.sb(f"{tagp}_xt{j}", [128, 1024], F32, sc) for j in range(2)]
        x1t = [p.sb(f"{tagp}_x1{j}", [128, 1024], F32, sc) for j in range(2)]
        for tb in range(TT // 128):
            j = tb % 2
            t0 = tt * TT + tb * 128
            for dblk in range(8):
                ka = (tagp, "af")
                p.op("sync", lambda s, dblk=dblk, t0=t0: s.dma_start(
                    out=af[:, :],
                    in_=attn_fm_hbm[dblk * 128:(dblk + 1) * 128, t0:t0 + 128]),
                    writes=[ka], dma="cbda")
                p.op("vector", lambda v, dblk=dblk, tb=tb: v.tensor_add(
                    yacc[:, dblk * TT + tb * 128: dblk * TT + (tb + 1) * 128],
                    yacc[:, dblk * TT + tb * 128: dblk * TT + (tb + 1) * 128],
                    af[:, :]),
                    reads=[ka, ("yacc",)], wlast=[("yacc",)])
            for dblk in range(8):
                bt, co = blocks[dblk]
                p.op("tensor", lambda t, dblk=dblk, tb=tb, bt=bt, co=co:
                     t.transpose(
                    bt[:, co:co + 128],
                    yacc[:, dblk * TT + tb * 128: dblk * TT + (tb + 1) * 128],
                    c["identf"][:, :]),
                    reads=[("yacc",), "identf"],
                    war=[(tagp, "pT")] if dblk == 0 else (),
                    writes=[(tagp, "pT")] if dblk == 7 else (), inc=(dblk == 7))
            p.op("sync", lambda s, t0=t0, j=j: s.dma_start(
                out=xt[j][:, :], in_=x_hbm[t0:t0 + 128, :]),
                writes=[(tagp, "x", j)], dma=f"cbdx{j}")
            for dblk in range(8):
                bt, co = blocks[dblk]
                p.op("vector", lambda v, j=j, dblk=dblk, bt=bt, co=co:
                     v.scalar_tensor_tensor(
                    x1t[j][:, dblk * 128:(dblk + 1) * 128], bt[:, co:co + 128],
                    1.0, xt[j][:, dblk * 128:(dblk + 1) * 128],
                    ALU.mult, ALU.add),
                    reads=[(tagp, "pT"), (tagp, "x", j)],
                    writes=[(tagp, "o", j)] if dblk == 0 else (),
                    wlast=[(tagp, "o", j)] if dblk else ())
            p.op("sync", lambda s, t0=t0, j=j: s.dma_start(
                out=out_hbm[t0:t0 + 128, :], in_=x1t[j][:, :]),
                reads=[(tagp, "o", j)], dma=f"cbdo{j}")
        p.barrier()



def emit_l1_proj(p, ws, xn_fm, T, hbm, c, tails_hbm=None, xnkey="xnfm", tag="l1p",
                 bufs=None):
    with ExitStack() as sc:
        ob = [p.sb(f"{tag}_ob{j}", [128, 512], BF16, sc) for j in range(4)]

        def cb_sg(p, ps, tb, nb, key):
            j = (tb * 3 + nb) % 4
            kb = (tag, "ob", j)
            if nb == 0:
                p.op("vector", lambda v, ps=ps, j=j: v.tensor_scalar(
                    ob[j][:, 0:256], ps[:, 0:256], 32.0, -32.0, ALU.min, ALU.max),
                    reads=[key], writes=[kb])
                p.op("sync", lambda s, tb=tb, j=j: s.dma_start(
                    out=hbm["s_tm"][tb * 128:(tb + 1) * 128, :],
                    in_=ob[j][:, 0:256]), reads=[kb], dma=f"lpst{j}")
            else:
                p.op("vector", lambda v, ps=ps, j=j: v.tensor_copy(
                    ob[j][:, :], ps[:, :]), reads=[key], writes=[kb])
                p.op("sync", lambda s, tb=tb, nb=nb, j=j: s.dma_start(
                    out=hbm["g_tm"][tb * 128:(tb + 1) * 128,
                                    (nb - 1) * 512: nb * 512],
                    in_=ob[j][:, :]), reads=[kb], dma=f"lpst{j}")
        emit_proj_tm(p, ws["wsg"], 1536, xn_fm, T, cb_sg, tag + "sg", c, pp=1, xnkey=xnkey)

        if bufs is None:
            bufs = {"wsb": p.sb(f"{tag}_w", [128, 8 * 128], BF16, sc),
                    "oq": [p.sb(f"{tag}_o{j}", [128, 512], BF16, sc)
                           for j in range(2)]}
        for nm, w_hbm, out_hbm, nmt in (("q", ws["wq"], hbm["qpre"], 4),
                                        ("k", ws["wk"], hbm["kpre"], 4),
                                        ("v", ws["wv"], hbm["vpre"], 8)):
            tg = "l1pf" + nm
            wsb = bufs["wsb"]
            pq = [p.ps(f"{tag}{nm}_p{j}", [128, 512], F32, sc) for j in range(2)]
            oq = bufs["oq"]
            W = min(512, T)
            for m in range(nmt):
                p.op("gpsimd", lambda g, m=m, wsb=wsb, w_hbm=w_hbm: g.dma_start(
                    out=wsb[:, :].rearrange("p (k f) -> p k f", k=8),
                    in_=w_hbm[:, m * 128:(m + 1) * 128].rearrange(
                        "(k p) f -> p k f", p=128)),
                    writes=[("l1pw",)], dma="l1w")
                for tb in range(T // W):
                    j = tb % 2
                    key = (tg, "p", j)
                    for k in range(8):
                        p.op("tensor", lambda t, m=m, tb=tb, k=k, j=j, wsb=wsb,
                             pq=pq, W=W: t.matmul(
                            pq[j][:, 0:W], wsb[:, k * 128:(k + 1) * 128],
                            xn_fm[:, k * T + tb * W: k * T + (tb + 1) * W],
                            start=(k == 0), stop=(k == 7)),
                            reads=[("l1pw",), (xnkey, k)],
                            war=[key] if k == 0 else (),
                            writes=[key] if k == 7 else (), inc=(k == 7))
                    p.op("vector", lambda v, j=j, oq=oq, pq=pq, W=W: v.tensor_copy(
                        oq[j][:, 0:W], pq[j][:, 0:W]), reads=[key],
                        writes=[("l1po", j)])
                    p.op("sync", lambda s, m=m, tb=tb, j=j, oq=oq, W=W,
                         out_hbm=out_hbm: s.dma_start(
                        out=out_hbm[m * 128:(m + 1) * 128, tb * W:(tb + 1) * W],
                        in_=oq[j][:, 0:W]), reads=[("l1po", j)], dma=f"qko{j}")
                    if tails_hbm is not None and tb == T // W - 1:
                        p.op("sync", lambda s, m=m, j=j, oq=oq, W=W,
                             nm=nm: s.dma_start(
                            out=tails_hbm[nm][m * 128:(m + 1) * 128, 0:3],
                            in_=oq[j][:, W - 3:W]), reads=[("l1po", j)],
                            dma="l1t")
        p.barrier()


def emit_l1_conv(p, hbm, T, conv_d, tails_in, cs_tabs, out_hbm, c):
    with ExitStack() as sc:
        W = min(512, T)
        xw = [p.sb(f"cv_x{j}", [128, W + 3], BF16, sc) for j in range(2)]
        cd = p.sb("cv_cd", [128, 4 * 128], BF16, sc)
        pc = [p.ps(f"cv_p{j}", [128, W], F32, sc) for j in range(2)]
        cv = [p.sb(f"cv_c{j}", [128, W], BF16, sc) for j in range(2)]
        rt = p.ps("cv_rt", [128, W], F32, sc)
        cst = p.sb("cv_cs", [128, 2 * W], BF16, sc)
        t1 = p.sb("cv_t1", [128, W], F32, sc)
        ro = [p.sb(f"cv_ro{j}", [128, W], BF16, sc) for j in range(2)]
        ptm = p.ps("cv_ptm", [128, 128], BF16, sc)
        otm = p.sb("cv_otm", [128, 128], BF16, sc)
        rope_mat = p.sb("cv_rm", [128, 128], BF16, sc)
        rmd = p.nc.dram_tensor("rope_mat", [128, 128], BF16, kind="ExternalInput")
        p.op("sync", lambda s: s.dma_start(out=rope_mat[:, :], in_=rmd[:, :]),
             writes=["ropem"], dma="cvcs")
        specs = [("q", hbm["qpre"], out_hbm["q_fm"], 4, 0, True),
                 ("k", hbm["kpre"], out_hbm["k_fm"], 4, 4, True),
                 ("v", hbm["vpre"], out_hbm["v_fm"], 8, 8, False)]
        for nm, src, dst, nmt, mg0, do_rope in specs:
            row0 = {"q": 0, "k": 512, "v": 1024}[nm]
            for m in range(nmt):
                mg = mg0 + m
                p.op("gpsimd", lambda g, mg=mg: g.dma_start(
                    out=cd[:, :].rearrange("p (a f) -> p a f", a=4),
                    in_=conv_d[mg].rearrange("a p f -> p a f")),
                    writes=[("cvd",)], dma="cvldd")
                for tb in range(T // W):
                    j = tb % 2
                    kx = ("cvx", j)
                    if tb == 0:
                        p.op("sync", lambda s, m=m, j=j, row0=row0: s.dma_start(
                            out=xw[j][:, 0:3],
                            in_=tails_in[row0 + m * 128: row0 + (m + 1) * 128, 0:3]),
                            writes=[kx], dma=f"cvt{j}")
                    else:
                        p.op("sync", lambda s, m=m, tb=tb, j=j, src=src: s.dma_start(
                            out=xw[j][:, 0:3],
                            in_=src[m * 128:(m + 1) * 128, tb * W - 3: tb * W]),
                            writes=[kx], dma=f"cvt{j}")
                    p.op("sync", lambda s, m=m, tb=tb, j=j, src=src: s.dma_start(
                        out=xw[j][:, 3:W + 3],
                        in_=src[m * 128:(m + 1) * 128, tb * W:(tb + 1) * W]),
                        writes=[kx], dma=f"cvx{j}")
                    for tap in range(4):
                        p.op("tensor", lambda t, tap=tap, j=j: t.matmul(
                            pc[j][:, :], cd[:, tap * 128:(tap + 1) * 128],
                            xw[j][:, tap:tap + W], start=(tap == 0), stop=(tap == 3)),
                            reads=[("cvd",), kx], war=[("cvp", j)] if tap == 0 else (),
                            writes=[("cvp", j)] if tap == 3 else (), inc=(tap == 3))
                    p.op("scalar", lambda s, j=j: s.activation(
                        cv[j][:, :], pc[j][:, :], AF.Silu),
                        reads=[("cvp", j)], writes=[("cvc", j)])
                    if do_rope:
                        qofs = 0 if nm == "q" else 2
                        p.op("sync", lambda s, tb=tb, qofs=qofs: s.dma_start(
                            out=cst[:, 0:W],
                            in_=cs_tabs[qofs][:, tb * W:(tb + 1) * W]),
                            writes=[("cvcs",)], dma="cvcs")
                        p.op("sync", lambda s, tb=tb, qofs=qofs: s.dma_start(
                            out=cst[:, W:2 * W],
                            in_=cs_tabs[qofs + 1][:, tb * W:(tb + 1) * W]),
                            writes=[("cvcs",)], dma="cvcs")
                        p.op("tensor", lambda t, j=j: t.matmul(
                            rt[:, :], rope_mat[:, :], cv[j][:, :],
                            start=True, stop=True),
                            reads=["ropem", ("cvc", j)], war=[("cvrt",)],
                            writes=[("cvrt",)])
                        p.op("vector", lambda v, j=j: v.tensor_mul(
                            t1[:, :], cv[j][:, :], cst[:, 0:W]),
                            reads=[("cvc", j), ("cvcs",)], writes=[("cvt1",)])
                        p.op("vector", lambda v, j=j: v.scalar_tensor_tensor(
                            ro[j][:, :], rt[:, :], 1.0, t1[:, :], ALU.mult,
                            ALU.add) if False else v.tensor_mul(
                            ro[j][:, :], rt[:, :], cst[:, W:2 * W]),
                            reads=[("cvrt",), ("cvcs",)], writes=[("cvro", j)])
                        p.op("vector", lambda v, j=j: v.tensor_add(
                            ro[j][:, :], ro[j][:, :], t1[:, :]),
                            reads=[("cvro", j), ("cvt1",)], writes=[("cvro", j)])
                        res = ro
                        kres = ("cvro", j)
                    else:
                        res = cv
                        kres = ("cvc", j)
                    p.op("sync", lambda s, m=m, tb=tb, j=j, dst=dst, res=res:
                         s.dma_start(
                        out=dst[m * 128:(m + 1) * 128, tb * W:(tb + 1) * W],
                        in_=res[j][:, 0:W]), reads=[kres], dma=f"dcvo{j}")
                    if nm in ("k", "v"):
                        tmdst = out_hbm["k_tm"] if nm == "k" else out_hbm["v_tm"]
                        for sb_ in range(W // 128):
                            p.op("tensor", lambda t, sb_=sb_, j=j, res=res:
                                 t.transpose(
                                ptm[:, :], res[j][:, sb_ * 128:(sb_ + 1) * 128],
                                c["identb"][:, :]),
                                reads=[kres, "identb"], war=[("cvptm",)],
                                writes=[("cvptm",)])
                            p.op("vector", lambda v: v.tensor_copy(
                                otm[:, :], ptm[:, :]), reads=[("cvptm",)],
                                writes=[("cvotm",)])
                            p.op("sync", lambda s, m=m, tb=tb, sb_=sb_,
                                 tmdst=tmdst: s.dma_start(
                                out=tmdst[tb * W + sb_ * 128:
                                          tb * W + (sb_ + 1) * 128,
                                          m * 128:(m + 1) * 128],
                                in_=otm[:, :]), reads=[("cvotm",)], dma="dcvtm")
        p.barrier()


def emit_l1_scan(p, hbm, T, carry_in, c, lt128_dram, attn_out=None,
                 wo_hbm=None, gnw4_dram=None, mask4_dram=None, carry_out=None,
                 full=True, tag="s1"):
    NCH = T // 128
    TTW = min(512, T)
    TCH = TTW // 128
    with ExitStack() as sc:
        lt = p.sb(f"{tag}_lt", [128, 128], BF16, sc)
        p.op("sync", lambda s: s.dma_start(out=lt[:, :], in_=lt128_dram[:, :]),
             writes=["lt128"], dma=f"{tag}cl")
        csum = p.sb(f"{tag}_csum", [1, 256], F32, sc)
        csumb = p.sb(f"{tag}_csumb", [1, 256], BF16, sc)
        HK = p.sb(f"{tag}_HK", [128, 256], F32, sc)
        HKb = p.sb(f"{tag}_HKb", [128, 256], BF16, sc)
        HV = p.sb(f"{tag}_HV", [64, 1024], F32, sc)
        HVb = p.sb(f"{tag}_HVb", [64, 1024], BF16, sc)
        onesc = p.sb(f"{tag}_onesc", [128, 1], BF16, sc)
        p.op("gpsimd", lambda g: g.memset(onesc[:, :], 1.0), writes=["onesc"])
        kc = [p.sb(f"{tag}_kc{j}", [128, 512], BF16, sc) for j in range(2)]
        vc = [p.sb(f"{tag}_vc{j}", [128, 1024], BF16, sc) for j in range(2)]
        sc_ = [p.sb(f"{tag}_sc{j}", [128, 256], BF16, sc) for j in range(2)]
        es = p.sb(f"{tag}_es", [128, 256], BF16, sc)
        pcumt = p.ps(f"{tag}_pcum", [128, 512], F32, sc)
        cums = p.sb(f"{tag}_cums", [128, 256], F32, sc)
        pHK = p.ps(f"{tag}_pHK", [128, 256], F32, sc)
        pHV = p.ps(f"{tag}_pHV", [64, 512], F32, sc)
        if full:
            qc = [p.sb(f"{tag}_qc{j}", [128, 512], BF16, sc) for j in range(2)]
            kfc = [p.sb(f"{tag}_kfc{j}", [128, 512], BF16, sc) for j in range(2)]
            gc = [p.sb(f"{tag}_gc{j}", [128, 1024], BF16, sc) for j in range(2)]
            mask4 = p.sb(f"{tag}_mask", [128, 512], BF16, sc)
            p.op("sync", lambda s: s.dma_start(out=mask4[:, :], in_=mask4_dram[:, :]),
                 writes=["mask4"], dma=f"{tag}cl")
            gnw4r = p.sb(f"{tag}_gnw4r", [1, 1024], BF16, sc)
            p.op("gpsimd", lambda g: g.dma_start(out=gnw4r[:, :], in_=gnw4_dram[:, :]),
                 writes=["gnw4r"], dma=f"{tag}cg")
            gnwt = p.sb(f"{tag}_gnwt", [128, 1024], BF16, sc)
            wo = p.sb(f"{tag}_wo", [128, 8 * 1024], BF16, sc)
            p.batch_sems.add(f"{tag}cg")
            for k in range(8):
                p.op("gpsimd", lambda g, k=k: g.dma_start(
                    out=wo[:, k * 1024:(k + 1) * 1024],
                    in_=wo_hbm[k * 128:(k + 1) * 128, :]), writes=["wo"], dma=f"{tag}cg")
            with ExitStack() as sc2:
                pnw = p.ps(f"{tag}_pnw", [128, 1024], F32, sc2)
                for hf in range(2):
                    p.op("tensor", lambda t, hf=hf: t.matmul(
                        pnw[:, hf * 512:(hf + 1) * 512], onesc[0:1, 0:1].rearrange(
                            "a b -> a b") if False else c["onesb"][0:1, 0:128],
                        gnw4r[0:1, hf * 512:(hf + 1) * 512], start=True, stop=True),
                        reads=["gnw4r", "onesb"],
                        writes=[("gnwp",)] if hf else ())
                p.op("vector", lambda v: v.tensor_copy(gnwt[:, :], pnw[:, :]),
                     reads=[("gnwp",)], writes=["gnwt"])
                p.barrier()
            pawk = p.ps(f"{tag}_pawk", [128, 512], F32, sc)
            am = p.sb(f"{tag}_am", [128, 512], BF16, sc)
            R = p.sb(f"{tag}_R", [128, 256], F32, sc)
            okr = p.sb(f"{tag}_okr", [128, 256], F32, sc)
            mx = p.sb(f"{tag}_mx", [128, 8], F32, sc)
            qv = p.sb(f"{tag}_qv", [128, 256], BF16, sc)
            pQT = p.ps(f"{tag}_pQT", [64, 1024], BF16, sc)
            esf = p.sb(f"{tag}_esf", [64, 512], BF16, sc)
            qvf = p.sb(f"{tag}_qvf", [64, 512], BF16, sc)
            wm = p.sb(f"{tag}_wm", [128, 512], BF16, sc)
            pov = p.ps(f"{tag}_pov", [128, 512], F32, sc)
            osq = p.sb(f"{tag}_osq", [128, 512], F32, sc)
            rsd = p.sb(f"{tag}_rsd", [128, 8], F32, sc)
            on = p.sb(f"{tag}_on", [128, 512], F32, sc)
            og = p.sb(f"{tag}_og", [128, 1024], BF16, sc)
            pT = p.ps(f"{tag}_pT", [128, 128], BF16, sc)
            ogfm = p.sb(f"{tag}_ogfm", [128, 8 * TTW], BF16, sc)
            pWO = p.ps(f"{tag}_pWO", [128, 512], F32, sc)
            ow = [p.sb(f"{tag}_ow{j}", [128, TTW], F32, sc) for j in range(2)]
        p.op("sync", lambda s: s.dma_start(out=csum[:, :], in_=carry_in[0:1, 0:256]),
             writes=["csum"], dma=f"{tag}cl")
        p.op("sync", lambda s: s.dma_start(out=HK[:, :], in_=carry_in[1:129, 0:256]),
             writes=["HK"], dma=f"{tag}cl")
        p.op("sync", lambda s: s.dma_start(out=HV[:, :], in_=carry_in[129:193, :]),
             writes=["HV"], dma=f"{tag}cl")
        for nm2, t_, k_ in (("csumb", csumb, "csum"), ("HKb", HKb, "HK"),
                            ("HVb", HVb, "HV")):
            p.op("vector", lambda v, t_=t_, nm2=nm2, k_=k_: v.tensor_copy(
                t_[:, :], {"csumb": csum, "HKb": HK, "HVb": HV}[nm2][:, :]),
                reads=[k_], writes=[nm2])
        for n in range(NCH):
            j = n % 2
            tt, cc2 = n // TCH, n % TCH
            kk, kv2, ks2 = ("s1k", j), ("s1v", j), ("s1s", j)
            p.op("sync", lambda s, n=n, j=j: s.dma_start(
                out=kc[j][:, :], in_=hbm["k_tm"][n * 128:(n + 1) * 128, :]),
                writes=[kk], dma=f"{tag}ld{j}")
            p.op("sync", lambda s, n=n, j=j: s.dma_start(
                out=vc[j][:, :], in_=hbm["v_tm"][n * 128:(n + 1) * 128, :]),
                writes=[kv2], dma=f"{tag}ld{j}")
            p.op("sync", lambda s, n=n, j=j: s.dma_start(
                out=sc_[j][:, :], in_=hbm["s_tm"][n * 128:(n + 1) * 128, :]),
                writes=[ks2], dma=f"{tag}ld{j}")
            if full:
                p.op("sync", lambda s, n=n, j=j: s.dma_start(
                    out=qc[j][:, :].rearrange("p (a t) -> p a t", a=4),
                    in_=hbm["q_fm"][:, n * 128:(n + 1) * 128]
                    .rearrange("(a p) t -> p a t", p=128)),
                    writes=[("s1q", j)], dma=f"{tag}ld{j}")
                p.op("sync", lambda s, n=n, j=j: s.dma_start(
                    out=kfc[j][:, :].rearrange("p (a t) -> p a t", a=4),
                    in_=hbm["k_fm"][:, n * 128:(n + 1) * 128]
                    .rearrange("(a p) t -> p a t", p=128)),
                    writes=[("s1kf", j)], dma=f"{tag}ld{j}")
                p.op("sync", lambda s, n=n, j=j: s.dma_start(
                    out=gc[j][:, :], in_=hbm["g_tm"][n * 128:(n + 1) * 128, :]),
                    writes=[("s1g", j)], dma=f"{tag}ld{j}")
            p.op("scalar", lambda s, j=j: s.activation(
                es[:, :], sc_[j][:, :], AF.Exp, bias=c["m32"][:, 0:1]),
                reads=[ks2, "m32"], writes=[("es",)])
            p.op("tensor", lambda t: t.matmul(
                pcumt[:, 0:256], lt[:, :], es[:, :], start=True, stop=False),
                reads=["lt128", ("es",)], war=[("pcum",)], inc=False)
            p.op("tensor", lambda t: t.matmul(
                pcumt[:, 0:256], c["onesb"][0:1, 0:128], csumb[0:1, :],
                start=False, stop=True),
                reads=["csumb", "onesb"], inc=False)
            p.op("tensor", lambda t: t.matmul(
                pcumt[0:1, 256:512], onesc[:, 0:1], es[:, :],
                start=True, stop=True),
                reads=[("es",), "onesc"], writes=[("pcum",)])
            p.op("vector", lambda v: v.tensor_copy(cums[:, :], pcumt[:, 0:256]),
                 reads=[("pcum",)], writes=[("cums",)])
            p.op("vector", lambda v: v.tensor_add(csum[:, :], csum[:, :],
                                                  pcumt[0:1, 256:512]),
                 reads=[("pcum",), "csum"], writes=["csum"])
            p.op("vector", lambda v: v.tensor_copy(csumb[:, :], csum[:, :]),
                 reads=["csum"], writes=["csumb"])
            if full:
                for h in range(4):
                    p.op("tensor", lambda t, h=h, j=j: t.matmul(
                        pawk[:, h * 128:(h + 1) * 128],
                        kfc[j][:, h * 128:(h + 1) * 128],
                        qc[j][:, h * 128:(h + 1) * 128], start=True, stop=True),
                        reads=[("s1kf", j), ("s1q", j)],
                        war=[("pawk",)] if h == 0 else (),
                        writes=[("pawk",)] if h == 3 else (), inc=(h == 3))
                p.op("vector", lambda v: v.tensor_mul(am[:, :], pawk[:, :],
                                                      mask4[:, :]),
                     reads=[("pawk",), "mask4"], writes=[("am",)])
                for h in range(4):
                    p.op("tensor", lambda t, h=h: t.matmul(
                        pawk[:, h * 64:(h + 1) * 64], am[:, h * 128:(h + 1) * 128],
                        es[:, h * 64:(h + 1) * 64], start=True, stop=False),
                        reads=[("am",), ("es",)], war=[("pawk",)] if h == 0 else (),
                        inc=False)
                    p.op("tensor", lambda t, h=h, j=j: t.matmul(
                        pawk[:, h * 64:(h + 1) * 64],
                        qc[j][:, h * 128:(h + 1) * 128],
                        HKb[:, h * 64:(h + 1) * 64], start=False, stop=True),
                        reads=[("s1q", j), "HKb"],
                        writes=[("pawk",)] if h == 3 else (), inc=(h == 3))
                p.op("vector", lambda v: v.reciprocal(R[:, :], cums[:, :]),
                     reads=[("cums",)], writes=[("R",)])
                p.op("vector", lambda v: v.tensor_mul(
                    okr[:, :], pawk[:, 0:256], R[:, :]),
                    reads=[("pawk",), ("R",)], writes=[("okr",)])
                p.op("vector", lambda v: v.reduce_max(
                    mx[:, 0:4], okr[:, :].rearrange("p (h m) -> p h m", h=4),
                    axis=AXX), reads=[("okr",)], writes=[("mx",)])
                p.op("vector", lambda v: v.tensor_sub(
                    okr[:, :].rearrange("p (h m) -> p h m", h=4),
                    okr[:, :].rearrange("p (h m) -> p h m", h=4),
                    bass.AP(mx, 0, [[8, 128], [1, 4], [0, 64]])),
                    reads=[("okr",), ("mx",)], writes=[("okr",)])
                p.op("scalar", lambda s: s.activation(okr[:, :], okr[:, :], AF.Exp),
                     reads=[("okr",)], writes=[("okr",)])
                p.op("vector", lambda v: v.reduce_sum(
                    mx[:, 4:8], okr[:, :].rearrange("p (h m) -> p h m", h=4),
                    axis=AXX), reads=[("okr",)], writes=[("mxs",)])
                p.op("vector", lambda v: v.reciprocal(mx[:, 4:8], mx[:, 4:8]),
                     reads=[("mxs",)], writes=[("mxs",)])
                p.op("vector", lambda v: v.tensor_mul(
                    okr[:, :].rearrange("p (h m) -> p h m", h=4),
                    okr[:, :].rearrange("p (h m) -> p h m", h=4),
                    bass.AP(mx, 4, [[8, 128], [1, 4], [0, 64]])),
                    reads=[("okr",), ("mxs",)], writes=[("okr",)])
                p.op("vector", lambda v: v.tensor_mul(qv[:, :], okr[:, :], R[:, :]),
                     reads=[("okr",), ("R",)], writes=[("qv",)])
                for h in range(4):
                    p.op("tensor", lambda t, h=h: t.transpose(
                        pQT[:, h * 128:(h + 1) * 128], es[:, h * 64:(h + 1) * 64],
                        c["identb"][:, :]),
                        reads=[("es",), "identb"], war=[("pQT",)] if h == 0 else (),
                        inc=False)
                for h in range(4):
                    p.op("tensor", lambda t, h=h: t.transpose(
                        pQT[:, 512 + h * 128: 512 + (h + 1) * 128],
                        qv[:, h * 64:(h + 1) * 64],
                        c["identb"][:, :]),
                        reads=[("qv",), "identb"],
                        writes=[("pQT",)] if h == 3 else (), inc=(h == 3))
                p.op("vector", lambda v: v.tensor_copy(esf[:, :], pQT[:, 0:512]),
                     reads=[("pQT",)], writes=[("esf",)])
                p.op("vector", lambda v: v.tensor_copy(qvf[:, :], pQT[:, 512:1024]),
                     reads=[("pQT",)], writes=[("qvf",)])
                for h in range(4):
                    p.op("tensor", lambda t, h=h: t.matmul(
                        pawk[:, h * 128:(h + 1) * 128],
                        esf[:, h * 128:(h + 1) * 128],
                        qvf[:, h * 128:(h + 1) * 128], start=True, stop=True),
                        reads=[("esf",), ("qvf",)], war=[("pawk",)] if h == 0 else (),
                        writes=[("pawk",)] if h == 3 else (), inc=(h == 3))
                p.op("vector", lambda v: v.tensor_mul(wm[:, :], pawk[:, :],
                                                      mask4[:, :]),
                     reads=[("pawk",), "mask4"], writes=[("wm",)])
                pass
            if full:
                p.op("scalar", lambda s, j=j: s.activation(
                    gc[j][:, :], gc[j][:, :], AF.Silu),
                    reads=[("s1g", j)], writes=[("s1g", j)])
                for hf in range(2):
                    for hh in range(2):
                        h = hf * 2 + hh
                        p.op("tensor", lambda t, h=h, hh=hh, j=j: t.matmul(
                            pov[:, hh * 256:(hh + 1) * 256],
                            wm[:, h * 128:(h + 1) * 128],
                            vc[j][:, h * 256:(h + 1) * 256], start=True,
                            stop=False),
                            reads=[("wm",), kv2], war=[("pov",)] if hh == 0 else (),
                            inc=False)
                        p.op("tensor", lambda t, h=h, hh=hh: t.matmul(
                            pov[:, hh * 256:(hh + 1) * 256],
                            qvf[:, h * 128:(h + 1) * 128],
                            HVb[:, h * 256:(h + 1) * 256], start=False, stop=True),
                            reads=[("qvf",), "HVb"],
                            writes=[("pov",)] if hh == 1 else (), inc=(hh == 1))
                    p.op("scalar", lambda s, hf=hf: s.activation(
                        osq[:, :], pov[:, :], AF.Square),
                        reads=[("pov",)], writes=[("osq",)])
                    p.op("vector", lambda v, hf=hf: v.reduce_sum(
                        rsd[:, hf * 2: hf * 2 + 2],
                        osq[:, :].rearrange("p (h v) -> p h v", h=2),
                        axis=AXX), reads=[("osq",)], writes=[("rsd", hf)])
                    p.op("scalar", lambda s, hf=hf: s.activation(
                        rsd[:, hf * 2: hf * 2 + 2], rsd[:, hf * 2: hf * 2 + 2],
                        AF.Ln, bias=c["eps"][:, 0:1],
                        scale=1.0 / 256), reads=[("rsd", hf), "eps"],
                        writes=[("rsd", hf)])
                    p.op("scalar", lambda s, hf=hf: s.activation(
                        rsd[:, hf * 2: hf * 2 + 2], rsd[:, hf * 2: hf * 2 + 2],
                        AF.Exp, scale=-0.5),
                        reads=[("rsd", hf)], writes=[("rsd", hf)])
                    p.op("vector", lambda v, hf=hf: v.tensor_mul(
                        on[:, :].rearrange("p (h v) -> p h v", h=2),
                        pov[:, :].rearrange("p (h v) -> p h v", h=2),
                        bass.AP(rsd, hf * 2, [[8, 128], [1, 2], [0, 256]])),
                        reads=[("pov",), ("rsd", hf)], writes=[("on",)])
                    p.op("vector", lambda v, hf=hf: v.tensor_mul(
                        on[:, :], on[:, :], gnwt[:, hf * 512:(hf + 1) * 512]),
                        reads=[("on",), "gnwt"], writes=[("on",)])
                    p.op("vector", lambda v, j=j, hf=hf: v.tensor_mul(
                        og[:, hf * 512:(hf + 1) * 512], on[:, :],
                        gc[j][:, hf * 512:(hf + 1) * 512]),
                        reads=[("on",), ("s1g", j)],
                        writes=[("og",)] if hf == 0 else (),
                        wlast=[("og",)] if hf else ())
                for vb in range(8):
                    p.op("tensor", lambda t, vb=vb: t.transpose(
                        pT[:, :], og[:, vb * 128:(vb + 1) * 128],
                        c["identb"][:, :]),
                        reads=[("og",), "identb"], war=[("pT1",)],
                        writes=[("pT1",)])
                    first = (cc2 == 0 and vb == 0)
                    p.op("vector", lambda v, vb=vb, cc2=cc2: v.tensor_copy(
                        ogfm[:, vb * TTW + cc2 * 128: vb * TTW + (cc2 + 1) * 128],
                        pT[:, :]),
                        reads=[("pT1",)],
                        writes=[("ogfm1",)] if first else (),
                        wlast=[("ogfm1",)] if not first else ())
                if cc2 == TCH - 1:
                    for dblk in range(8):
                        jj = dblk % 2
                        for k in range(8):
                            p.op("tensor", lambda t, dblk=dblk, k=k: t.matmul(
                                pWO[:, 0:TTW],
                                wo[:, k * 1024 + dblk * 128:
                                   k * 1024 + (dblk + 1) * 128],
                                ogfm[:, k * TTW:(k + 1) * TTW],
                                start=(k == 0), stop=(k == 7)),
                                reads=["wo", ("ogfm1",)],
                                war=[("pWO1",)] if k == 0 else (),
                                writes=[("pWO1",)] if k == 7 else (), inc=(k == 7))
                        p.op("vector", lambda v, jj=jj: v.tensor_copy(
                            ow[jj][:, :], pWO[:, 0:TTW]), reads=[("pWO1",)],
                            writes=[("ow1", jj)])
                        p.op("sync", lambda s, tt=tt, dblk=dblk, jj=jj: s.dma_start(
                            out=attn_out[dblk * 128:(dblk + 1) * 128,
                                         tt * TTW:(tt + 1) * TTW],
                            in_=ow[jj][:, :]),
                            reads=[("ow1", jj)], dma=f"s1da{jj}")
            for h in range(4):
                p.op("tensor", lambda t, h=h, j=j: t.matmul(
                    pHK[:, h * 64:(h + 1) * 64], kc[j][:, h * 128:(h + 1) * 128],
                    es[:, h * 64:(h + 1) * 64], start=True, stop=True),
                    reads=[kk, ("es",)], war=[("pHK",)] if h == 0 else (),
                    writes=[("pHK",)] if h == 3 else (), inc=(h == 3))
            p.op("vector", lambda v: v.tensor_add(HK[:, :], HK[:, :], pHK[:, :]),
                 reads=[("pHK",), "HK", "HKb"], writes=["HK"])
            p.op("vector", lambda v: v.tensor_copy(HKb[:, :], HK[:, :]),
                 reads=["HK"], writes=["HKb"])
            for hf in range(2):
                for hh in range(2):
                    h = hf * 2 + hh
                    p.op("tensor", lambda t, h=h, hh=hh, j=j: t.matmul(
                        pHV[:, hh * 256:(hh + 1) * 256], es[:, h * 64:(h + 1) * 64],
                        vc[j][:, h * 256:(h + 1) * 256], start=True, stop=True),
                        reads=[("es",), kv2], war=[("pHV",)] if hh == 0 else (),
                        writes=[("pHV",)] if hh == 1 else (), inc=(hh == 1))
                p.op("vector", lambda v, hf=hf: v.tensor_add(
                    HV[:, hf * 512:(hf + 1) * 512], HV[:, hf * 512:(hf + 1) * 512],
                    pHV[:, :]),
                    reads=[("pHV",), "HV", "HVb"], writes=["HV"])
            p.op("vector", lambda v: v.tensor_copy(HVb[:, :], HV[:, :]),
                 reads=["HV"], writes=["HVb"])
        if carry_out is not None:
            p.op("sync", lambda s: s.dma_start(out=carry_out[0:1, 0:256],
                                               in_=csum[:, :]),
                 reads=["csum"], dma="dco0")
            p.op("sync", lambda s: s.dma_start(out=carry_out[1:129, 0:256],
                                               in_=HK[:, :]),
                 reads=["HK"], dma="dco1")
            p.op("sync", lambda s: s.dma_start(out=carry_out[129:193, :],
                                               in_=HV[:, :]),
                 reads=["HV"], dma="dco2")
        p.barrier()


def emit_l1_router(p, rwt_dram, rb_dram, xn_fm, T, comb_dram, c, tagp="rt1",
                   xnkey="xnfm"):
    with ExitStack() as sc:
        lg = p.sb(f"{tagp}_lg", [128, 8], F32, sc)
        st = p.sb(f"{tagp}_st", [128, 6], F32, sc)
        e = p.sb(f"{tagp}_e", [128, 8], F32, sc)
        ge2 = p.sb(f"{tagp}_g2", [128, 8], F32, sc)
        cmb = p.sb(f"{tagp}_cmb", [128, 8], BF16, sc)
        pcT = p.ps(f"{tagp}_pcT", [8, 128], BF16, sc)
        cTs = [p.sb(f"{tagp}_cTs{j}", [8, 128], BF16, sc) for j in range(2)]

        def cb(p, ps, tb, nb, key):
            kl, ke = (tagp, "lg"), (tagp, "ebuf")
            p.op("vector", lambda v, ps=ps: v.tensor_copy(lg[:, :], ps[:, 0:8]),
                 reads=[key], writes=[kl])
            p.op("vector", lambda v: v.reduce_max(st[:, 0:1], lg[:, :], axis=AXX),
                 reads=[kl], writes=[(tagp, "m1")])
            p.op("vector", lambda v: v.tensor_scalar_mul(st[:, 2:3], st[:, 0:1], -1.0),
                 reads=[(tagp, "m1")], writes=[(tagp, "nm1")])
            p.op("vector", lambda v: v.tensor_scalar(
                e[:, :], lg[:, :], st[:, 0:1], None, ALU.is_ge),
                reads=[kl, (tagp, "m1")], writes=[ke])
            p.op("vector", lambda v: v.scalar_tensor_tensor(
                e[:, :], e[:, :], -1e30, lg[:, :], ALU.mult, ALU.add),
                reads=[ke, kl], writes=[ke])
            p.op("vector", lambda v: v.reduce_max(st[:, 1:2], e[:, :], axis=AXX),
                 reads=[ke], writes=[(tagp, "m2")])
            p.op("vector", lambda v: v.tensor_scalar(
                ge2[:, :], lg[:, :], st[:, 1:2], None, ALU.is_ge),
                reads=[kl, (tagp, "m2")], writes=[(tagp, "ge2")])
            p.op("scalar", lambda s: s.activation(
                e[:, :], lg[:, :], AF.Exp, bias=st[:, 2:3]),
                reads=[kl, (tagp, "nm1")], writes=[ke])
            p.op("vector", lambda v: v.reduce_sum(st[:, 4:5], e[:, :], axis=AXX),
                 reads=[ke], writes=[(tagp, "z8")])
            p.op("scalar", lambda s: s.activation(
                st[:, 3:4], st[:, 1:2], AF.Exp, bias=st[:, 2:3]),
                reads=[(tagp, "m2"), (tagp, "nm1")], writes=[(tagp, "dn")])
            p.op("vector", lambda v: v.scalar_tensor_tensor(
                st[:, 3:4], st[:, 4:5], 1e-9, st[:, 3:4], ALU.mult, ALU.add),
                reads=[(tagp, "z8"), (tagp, "dn")], writes=[(tagp, "dn")])
            p.op("vector", lambda v: v.tensor_scalar_add(st[:, 3:4], st[:, 3:4], 1.0),
                 reads=[(tagp, "dn")], writes=[(tagp, "dn")])
            p.op("vector", lambda v: v.reciprocal(st[:, 3:4], st[:, 3:4]),
                 reads=[(tagp, "dn")], writes=[(tagp, "dn")])
            p.op("vector", lambda v: v.scalar_tensor_tensor(
                cmb[:, :], e[:, :], st[:, 3:4], ge2[:, :], ALU.mult, ALU.mult),
                reads=[ke, (tagp, "dn"), (tagp, "ge2")], writes=[(tagp, "cmb")])
            p.op("tensor", lambda t: t.transpose(
                pcT[:, :], cmb[:, :], c["identb"][:, :]),
                reads=[(tagp, "cmb"), "identb"], war=[(tagp, "pcT")],
                writes=[(tagp, "pcT")])
            jj = tb % 2
            p.op("vector", lambda v, jj=jj: v.tensor_copy(
                cTs[jj][:, :], pcT[:, :]),
                reads=[(tagp, "pcT")], writes=[(tagp, "cTs", jj)])
            p.op("sync", lambda sy, tb=tb, jj=jj: sy.dma_start(
                out=comb_dram[:, tb * 128:(tb + 1) * 128], in_=cTs[jj][:, :]),
                reads=[(tagp, "cTs", jj)], dma=f"rtdc{jj}")
        emit_proj_tm(p, rwt_dram, 512, xn_fm, T, cb, tagp + "p", c,
                     bias_hbm=rb_dram, pp=1, xnkey=xnkey)
        p.barrier()



TC = 4096


def _mk_l0_weights(ins):
    z = np.zeros((1024, 496), np.float32)
    return {
        "wkv": np.ascontiguousarray(np.concatenate(
            [ins["L0_attn_Wk"], ins["L0_attn_Wv"]], 1).astype(np.float32)),
        "wggk": np.ascontiguousarray(np.concatenate(
            [ins["L0_attn_Wg"], ins["L0_attn_Wgk1"], z], 1).astype(np.float32)),
        "wgk2": _f32c(ins["L0_attn_Wgk2"]),
        "bgk2": _f32c(ins["L0_attn_bgk2"])[None, :],
        "wq": _f32c(ins["L0_attn_Wq"]), "wk": _f32c(ins["L0_attn_Wk"]),
        "wo": _f32c(ins["L0_attn_Wo"]),
        "nw8": np.ascontiguousarray(np.tile(_f32c(ins["L0_attn_norm_w"]), 8))[None, :],
        "lt64": np.ascontiguousarray((np.triu(np.ones((64, 64))) / 16.0)
                                     .astype(np.float32)),
        "mask8": np.ascontiguousarray(np.tile(
            np.triu(np.ones((64, 64), np.float32)), (1, 8)).astype(BF)),
        "identb": np.eye(128, dtype=np.float32).astype(BF),
        "identf": np.eye(128, dtype=np.float32),
    }


def _f32c(x):
    return np.ascontiguousarray(np.asarray(x, dtype=np.float32))


def _run(nc, in_maps):
    return run_bass_kernel_spmd(nc, in_maps, core_ids=list(range(NCORE)))


def build_launch_A(T=TC):
    nc = bass.Bass("TRN2", target_bir_lowering=False, debug=False)
    x = nc.dram_tensor("x", [T, 1024], F32, kind="ExternalInput")
    ws = {n: nc.dram_tensor(n, shp, F32, kind="ExternalInput") for n, shp in
          [("wkv", [1024, 1536]), ("wggk", [1024, 1536]), ("wgk2", [16, 512]),
           ("bgk2", [1, 512]), ("lt64", [64, 64])]}
    hbm = {"k_tm": nc.dram_tensor("k_tm", [T, 512], BF16, kind="ExternalOutput"),
           "v_tm": nc.dram_tensor("v_tm", [T, 1024], BF16, kind="ExternalOutput"),
           "g_tm": nc.dram_tensor("g_tm", [T, 1024], BF16, kind="ExternalOutput"),
           "gk_tm": nc.dram_tensor("gk_tm", [T, 512], F32, kind="ExternalOutput")}
    xnfm = nc.dram_tensor("xnfm", [1024, T], BF16, kind="ExternalOutput")
    s_out = nc.dram_tensor("s_out", [64, 1024], F32, kind="ExternalOutput")
    p = Prog(nc)
    c = emit_consts(p, nc)
    xn_fm = p.sb("xn_fm", [128, 8 * T], BF16)
    emit_ln_fm(p, x, T, xn_fm, c, "ln0", xnfm_hbm=xnfm)
    emit_l0_proj_tm(p, ws, xn_fm, T, hbm, c)
    emit_l0_pass1(p, hbm, T, s_out, c, ws["lt64"])
    p.barrier()
    return p.finalize()


def build_launch_B(T=TC):
    nc = bass.Bass("TRN2", target_bir_lowering=False, debug=False)
    names = [("wq", [1024, 512]), ("wk", [1024, 512]), ("wo", [1024, 1024]),
             ("nw8", [1, 1024]), ("s_in", [64, 1024]), ("lt64", [64, 64])]
    ws = {n: nc.dram_tensor(n, shp, F32, kind="ExternalInput") for n, shp in names}
    mask8 = nc.dram_tensor("mask8", [64, 512], BF16, kind="ExternalInput")
    xnfm = nc.dram_tensor("xnfm", [1024, T], BF16, kind="ExternalInput")
    hbm = {"k_tm": nc.dram_tensor("k_tm", [T, 512], BF16, kind="ExternalInput"),
           "v_tm": nc.dram_tensor("v_tm", [T, 1024], BF16, kind="ExternalInput"),
           "g_tm": nc.dram_tensor("g_tm", [T, 1024], BF16, kind="ExternalInput"),
           "gk_tm": nc.dram_tensor("gk_tm", [T, 512], F32, kind="ExternalInput")}
    attn = nc.dram_tensor("attn", [1024, T], F32, kind="ExternalOutput")
    p = Prog(nc)
    c = emit_consts(p, nc)
    q_hbm = nc.dram_tensor("q_fm_s", [512, T], BF16)
    k_hbm = nc.dram_tensor("k_fm_s", [512, T], BF16)
    with ExitStack() as sx:
        xn_fm = p.sb("xn_fm", [128, 8 * T], BF16, sx)
        emit_load_xnfm(p, xnfm, T, xn_fm)
        emit_l0_qk_fm(p, ws["wq"], ws["wk"], xn_fm, T, q_hbm, k_hbm)
        p.barrier()
    emit_l0_pass2(p, hbm, T, q_hbm, k_hbm, ws["s_in"], ws["wo"], ws["nw8"],
                  mask8, ws["lt64"], attn, c)
    p.barrier()
    return p.finalize()


def build_launch_C(T=TC, TT=512):
    nc = bass.Bass("TRN2", target_bir_lowering=False, debug=False)
    x = nc.dram_tensor("x", [T, 1024], F32, kind="ExternalInput")
    attn = nc.dram_tensor("attn", [1024, T], F32, kind="ExternalInput")
    xnfm = nc.dram_tensor("xnfm", [1024, T], BF16, kind="ExternalInput")
    rwt = nc.dram_tensor("rwt", [1024, 512], F32, kind="ExternalInput")
    gate = nc.dram_tensor("gate", [8, 1024, 4096], F32, kind="ExternalInput")
    up = nc.dram_tensor("up", [8, 1024, 4096], F32, kind="ExternalInput")
    down = nc.dram_tensor("down", [8, 4096, 1024], F32, kind="ExternalInput")
    x1 = nc.dram_tensor("x1", [T, 1024], F32, kind="ExternalOutput")
    comb_dram = nc.dram_tensor("combd", [8, T], BF16)
    p = Prog(nc)
    c = emit_consts(p, nc)
    xn_fm = p.sb("xn_fm", [128, 8 * T], BF16)
    yacc = p.sb("yacc", [128, 8 * TT], F32)
    hs = p.sb("hs", [128, 32 * TT], BF16)
    emit_load_xnfm(p, xnfm, T, xn_fm)
    emit_l0_router(p, rwt, xn_fm, T, comb_dram, c)
    fsc = ExitStack()
    ft = make_ffn_tiles(p, fsc, min(512, TT))
    for tt in range(T // TT):
        for e in range(8):
            emit_ffn_expert(p, f"fx{e}", ft, xn_fm, T, tt, TT, up[e], down[e],
                            hs, yacc, c, w1g_dram=gate[e], comb_dram=comb_dram,
                            e_idx=e, first=(e == 0))
        p.barrier()
        emit_combine_x1(p, yacc, tt, TT, T, x, attn, x1, c,
                        (ft["py"][0], ft["py"][1], ft["pg"][0], ft["pg"][1]),
                        tagp=f"cmb{tt}")
    fsc.close()
    p.barrier()
    return p.finalize()


def build_launch_D(T=TC):
    nc = bass.Bass("TRN2", target_bir_lowering=False, debug=False)
    x1 = nc.dram_tensor("x1", [T, 1024], F32, kind="ExternalInput")
    names = [("wq", [1024, 512]), ("wk", [1024, 512]), ("wv", [1024, 1024]),
             ("wsg", [1024, 1536]), ("carry_zero", [193, 1024])]
    ws = {n: nc.dram_tensor(n, shp, F32, kind="ExternalInput") for n, shp in names}
    convd = nc.dram_tensor("convd", [16, 4, 128, 128], BF16, kind="ExternalInput")
    cstabs = nc.dram_tensor("cstabs", [4, 128, T], BF16, kind="ExternalInput")
    lt128 = nc.dram_tensor("lt128", [128, 128], BF16, kind="ExternalInput")
    tails_zero = nc.dram_tensor("tails_zero", [2048, 4], BF16, kind="ExternalInput")
    hbm = {}
    outs = [("s_tm", [T, 256]), ("g_tm", [T, 1024]), ("qpre", [512, T]),
            ("kpre", [512, T]), ("vpre", [1024, T])]
    for n, shp in outs:
        hbm[n] = nc.dram_tensor(n, shp, BF16, kind="ExternalOutput")
    for n, shp in [("q_fm", [512, T]), ("k_fm", [512, T]), ("v_fm", [1024, T]),
                   ("k_tm", [T, 512]), ("v_tm", [T, 1024])]:
        hbm[n] = nc.dram_tensor(n, shp, BF16)
    xnfm1 = nc.dram_tensor("xnfm1", [1024, T], BF16, kind="ExternalOutput")
    tails = {nm: nc.dram_tensor(f"tails_{nm}", [shp, 4], BF16,
                                kind="ExternalOutput")
             for nm, shp in (("q", 512), ("k", 512), ("v", 1024))}
    carry_out = nc.dram_tensor("carry_out", [193, 1024], F32, kind="ExternalOutput")
    p = Prog(nc)
    c = emit_consts(p, nc)
    xn_fm = p.sb("xn_fm", [128, 8 * T], BF16)
    emit_ln_fm(p, x1, T, xn_fm, c, "ln1", xnfm_hbm=xnfm1)
    emit_l1_proj(p, ws, xn_fm, T, hbm, c, tails_hbm=tails)
    emit_l1_conv(p, hbm, T, convd, tails_zero, cstabs, hbm, c)
    emit_l1_scan(p, hbm, T, ws["carry_zero"], c, lt128, carry_out=carry_out,
                 full=False, tag="sp1")
    p.barrier()
    return p.finalize()


def build_launch_E(T=TC):
    nc = bass.Bass("TRN2", target_bir_lowering=False, debug=False)
    names = [("wo", [1024, 1024]), ("gnw4", [1, 1024]), ("carry_in", [193, 1024])]
    ws = {n: nc.dram_tensor(n, shp, F32, kind="ExternalInput") for n, shp in names}
    convd = nc.dram_tensor("convd", [16, 4, 128, 128], BF16, kind="ExternalInput")
    cstabs = nc.dram_tensor("cstabs", [4, 128, T], BF16, kind="ExternalInput")
    lt128 = nc.dram_tensor("lt128", [128, 128], BF16, kind="ExternalInput")
    mask4 = nc.dram_tensor("mask4", [128, 512], BF16, kind="ExternalInput")
    tails_in = nc.dram_tensor("tails_in", [2048, 4], BF16, kind="ExternalInput")
    hbm = {}
    for n, shp in [("s_tm", [T, 256]), ("g_tm", [T, 1024]), ("qpre", [512, T]),
                   ("kpre", [512, T]), ("vpre", [1024, T])]:
        hbm[n] = nc.dram_tensor(n, shp, BF16, kind="ExternalInput")
    for n, shp in [("q_fm", [512, T]), ("k_fm", [512, T]), ("v_fm", [1024, T]),
                   ("k_tm", [T, 512]), ("v_tm", [T, 1024])]:
        hbm[n] = nc.dram_tensor(n, shp, BF16)
    attn = nc.dram_tensor("attn", [1024, T], F32, kind="ExternalOutput")
    p = Prog(nc)
    c = emit_consts(p, nc)
    emit_l1_conv(p, hbm, T, convd, tails_in, cstabs, hbm, c)
    emit_l1_scan(p, hbm, T, ws["carry_in"], c, lt128, attn_out=attn,
                 wo_hbm=ws["wo"], gnw4_dram=ws["gnw4"], mask4_dram=mask4,
                 full=True)
    p.barrier()
    return p.finalize()


def build_launch_F(T=TC, TT=512):
    nc = bass.Bass("TRN2", target_bir_lowering=False, debug=False)
    x1 = nc.dram_tensor("x1", [T, 1024], F32, kind="ExternalInput")
    attn = nc.dram_tensor("attn", [1024, T], F32, kind="ExternalInput")
    xnfm = nc.dram_tensor("xnfm1", [1024, T], BF16, kind="ExternalInput")
    rwt = nc.dram_tensor("rwt", [1024, 512], F32, kind="ExternalInput")
    rbp = nc.dram_tensor("rbp", [1, 512], F32, kind="ExternalInput")
    sw1 = nc.dram_tensor("sw1", [1024, 4096], F32, kind="ExternalInput")
    sb1 = nc.dram_tensor("sb1", [128, 32], F32, kind="ExternalInput")
    sw2 = nc.dram_tensor("sw2", [4096, 1024], F32, kind="ExternalInput")
    sb2 = nc.dram_tensor("sb2", [1, 1024], F32, kind="ExternalInput")
    ew1 = nc.dram_tensor("ew1", [8, 1024, 4096], F32, kind="ExternalInput")
    eb1 = nc.dram_tensor("eb1", [8, 128, 32], F32, kind="ExternalInput")
    ew2 = nc.dram_tensor("ew2", [8, 4096, 1024], F32, kind="ExternalInput")
    eb2 = nc.dram_tensor("eb2", [8, 1, 1024], F32, kind="ExternalInput")
    x2 = nc.dram_tensor("x2", [T, 1024], F32, kind="ExternalOutput")
    comb_dram = nc.dram_tensor("combd", [8, T], BF16)
    p = Prog(nc)
    c = emit_consts(p, nc)
    xn_fm = p.sb("xn_fm", [128, 8 * T], BF16)
    yacc = p.sb("yacc", [128, 8 * TT], F32)
    hs = p.sb("hs", [128, 32 * TT], BF16)
    emit_load_xnfm(p, xnfm, T, xn_fm)
    emit_l1_router(p, rwt, rbp, xn_fm, T, comb_dram, c)
    fsc = ExitStack()
    ft = make_ffn_tiles(p, fsc, min(512, TT))
    for tt in range(T // TT):
        emit_ffn_expert(p, "fsh", ft, xn_fm, T, tt, TT, sw1, sw2, hs, yacc, c,
                        b1_dram=sb1, b2_dram=sb2, first=True, act=AF.Gelu)
        for e in range(8):
            emit_ffn_expert(p, f"fe{e}", ft, xn_fm, T, tt, TT, ew1[e], ew2[e],
                            hs, yacc, c, b1_dram=eb1[e], b2_dram=eb2[e],
                            comb_dram=comb_dram, e_idx=e, act=AF.Gelu)
        p.barrier()
        emit_combine_x1(p, yacc, tt, TT, T, x1, attn, x2, c,
                        (ft["py"][0], ft["py"][1], ft["pg"][0], ft["pg"][1]),
                        tagp=f"cm2{tt}")
    fsc.close()
    p.barrier()
    return p.finalize()


def kernel(**ins):
    B, Tfull = 4, 8192
    x = _f32c(ins["L0_attn_x"])
    l0w = _mk_l0_weights(ins)
    qcw = _f32c(ins["L1_attn_qconv_w"]); kcw = _f32c(ins["L1_attn_kconv_w"])
    vcw = _f32c(ins["L1_attn_vconv_w"])
    convd = np.zeros((16, 4, 128, 128), np.float32)
    for m in range(4):
        for tap in range(4):
            convd[m, tap] = np.diag(qcw[m * 128:(m + 1) * 128, 0, tap])
            convd[4 + m, tap] = np.diag(kcw[m * 128:(m + 1) * 128, 0, tap])
    for m in range(8):
        for tap in range(4):
            convd[8 + m, tap] = np.diag(vcw[m * 128:(m + 1) * 128, 0, tap])
    convd = convd.astype(BF)
    inv = 1.0 / 10000.0 ** (np.arange(0, 128, 2, dtype=np.float64) / 128)
    scale = 128.0 ** -0.5
    cstabs_h = []
    for hf in range(2):
        pos0 = hf * TC
        frq = (pos0 + np.arange(TC, dtype=np.float64))[:, None] * inv[None, :]
        emb = np.concatenate([frq, frq], -1)
        cstabs_h.append(np.ascontiguousarray(np.stack(
            [np.cos(emb).T * scale, np.sin(emb).T * scale,
             np.cos(emb).T, np.sin(emb).T]).astype(np.float32).astype(BF)))
    R = np.zeros((128, 128), np.float32)
    R[:64, 64:] = -np.eye(64); R[64:, :64] = np.eye(64)
    ropem = np.ascontiguousarray(R.T).astype(BF)
    lt128 = np.triu(np.ones((128, 128), np.float32)).astype(BF)
    mask4 = np.ascontiguousarray(np.tile(
        np.triu(np.ones((128, 128), np.float32)), (1, 4))).astype(BF)
    gnw4 = np.ascontiguousarray(np.tile(_f32c(ins["L1_attn_gnorm_w"]), 4))[None, :]
    wsg = np.ascontiguousarray(np.concatenate(
        [_f32c(ins["L1_attn_Ws"]), np.zeros((1024, 256), np.float32),
         _f32c(ins["L1_attn_Wg"])], 1))
    rwt0 = np.ascontiguousarray(np.concatenate(
        [_f32c(ins["L0_ffn_router_w"]).T, np.zeros((1024, 504), np.float32)], 1))
    rwt1 = np.ascontiguousarray(np.concatenate(
        [_f32c(ins["L1_ffn_router_w"]), np.zeros((1024, 504), np.float32)], 1))
    rbp1 = np.ascontiguousarray(np.concatenate(
        [_f32c(ins["L1_ffn_router_b"]), np.zeros(504, np.float32)]))[None, :]
    sb1 = np.ascontiguousarray(_f32c(ins["L1_ffn_shared_b1"]).reshape(32, 128).T)
    eb1 = np.ascontiguousarray(_f32c(ins["L1_ffn_exp_b1"]).reshape(8, 32, 128)
                               .transpose(0, 2, 1))
    shards = []
    for ci in range(NCORE):
        b, hf = ci // 2, ci % 2
        shards.append(np.ascontiguousarray(x[b, hf * TC:(hf + 1) * TC, :]))

    zero_carry = np.zeros((193, 1024), np.float32)
    zero_tails = np.zeros((2048, 4), np.float32).astype(BF)
    zero_s = np.zeros((64, 1024), np.float32)

    nc = build_launch_A()
    im = [dict(x=shards[ci], wkv=l0w["wkv"], wggk=l0w["wggk"], wgk2=l0w["wgk2"],
               bgk2=l0w["bgk2"], lt64=l0w["lt64"], identb=l0w["identb"],
               identf=l0w["identf"]) for ci in range(NCORE)]
    ra = [{k2: np.asarray(v2) for k2, v2 in r.items()} for r in _run(nc, im).results]
    del im
    nc = build_launch_B()
    im = []
    for ci in range(NCORE):
        s_in = zero_s if ci % 2 == 0 else np.asarray(ra[ci - 1]["s_out"])
        im.append(dict(wq=l0w["wq"], wk=l0w["wk"], wo=l0w["wo"], nw8=l0w["nw8"],
                       s_in=_f32c(s_in), lt64=l0w["lt64"], mask8=l0w["mask8"],
                       xnfm=np.asarray(ra[ci]["xnfm"]),
                       k_tm=np.asarray(ra[ci]["k_tm"]),
                       v_tm=np.asarray(ra[ci]["v_tm"]),
                       g_tm=np.asarray(ra[ci]["g_tm"]),
                       gk_tm=np.asarray(ra[ci]["gk_tm"]),
                       identb=l0w["identb"], identf=l0w["identf"]))
    rb = [{k2: np.asarray(r["attn"])} if False else {"attn": np.asarray(r["attn"])}
          for r in _run(nc, im).results]
    del im
    import gc
    gate_t = np.ascontiguousarray(_f32c(ins["L0_ffn_gate_w"]).transpose(0, 2, 1))
    up_t = np.ascontiguousarray(_f32c(ins["L0_ffn_up_w"]).transpose(0, 2, 1))
    down_t = np.ascontiguousarray(_f32c(ins["L0_ffn_down_w"]).transpose(0, 2, 1))
    nc = build_launch_C()
    im = [dict(x=shards[ci], attn=np.asarray(rb[ci]["attn"]),
               xnfm=np.asarray(ra[ci]["xnfm"]), rwt=rwt0, gate=gate_t, up=up_t,
               down=down_t, identb=l0w["identb"], identf=l0w["identf"])
          for ci in range(NCORE)]
    rc = [{"x1": np.asarray(r["x1"])} for r in _run(nc, im).results]
    del im, gate_t, up_t, down_t, rb
    gc.collect()
    nc = build_launch_D()
    im = [dict(x1=_f32c(np.asarray(rc[ci]["x1"])), wq=_f32c(ins["L1_attn_Wq"]),
               wk=_f32c(ins["L1_attn_Wk"]), wv=_f32c(ins["L1_attn_Wv"]), wsg=wsg,
               carry_zero=zero_carry, convd=convd, cstabs=cstabs_h[ci % 2],
               lt128=lt128, tails_zero=zero_tails, rope_mat=ropem,
               identb=l0w["identb"], identf=l0w["identf"])
          for ci in range(NCORE)]
    rd = [{k2: np.asarray(v2) for k2, v2 in r.items()} for r in _run(nc, im).results]
    del im
    xnfm1_list = [rd[ci]["xnfm1"] for ci in range(NCORE)]
    nc = build_launch_E()
    im = []
    for ci in range(NCORE):
        if ci % 2 == 0:
            carry, tails = zero_carry, zero_tails
        else:
            carry = _f32c(np.asarray(rd[ci - 1]["carry_out"]))
            tails = np.concatenate(
                [np.asarray(rd[ci - 1]["tails_q"]),
                 np.asarray(rd[ci - 1]["tails_k"]),
                 np.asarray(rd[ci - 1]["tails_v"])], 0)
            tails = np.ascontiguousarray(tails)
        im.append(dict(wo=_f32c(ins["L1_attn_Wo"]), gnw4=gnw4, carry_in=carry,
                       convd=convd, cstabs=cstabs_h[ci % 2], lt128=lt128,
                       mask4=mask4, tails_in=tails,
                       s_tm=np.asarray(rd[ci]["s_tm"]),
                       g_tm=np.asarray(rd[ci]["g_tm"]),
                       qpre=np.asarray(rd[ci]["qpre"]),
                       kpre=np.asarray(rd[ci]["kpre"]),
                       vpre=np.asarray(rd[ci]["vpre"]), rope_mat=ropem,
                       identb=l0w["identb"], identf=l0w["identf"]))
    re_ = [{"attn": np.asarray(r["attn"])} for r in _run(nc, im).results]
    del im, rd, ra
    gc.collect()
    nc = build_launch_F()
    im = [dict(x1=_f32c(np.asarray(rc[ci]["x1"])),
               attn=_f32c(np.asarray(re_[ci]["attn"])),
               xnfm1=xnfm1_list[ci], rwt=rwt1, rbp=rbp1,
               sw1=_f32c(ins["L1_ffn_shared_w1"]), sb1=sb1,
               sw2=_f32c(ins["L1_ffn_shared_w2"]),
               sb2=_f32c(ins["L1_ffn_shared_b2"])[None, :],
               ew1=_f32c(ins["L1_ffn_exp_w1"]), eb1=eb1,
               ew2=_f32c(ins["L1_ffn_exp_w2"]),
               eb2=_f32c(ins["L1_ffn_exp_b2"])[:, None, :],
               identb=l0w["identb"], identf=l0w["identf"])
          for ci in range(NCORE)]
    rf = [{"x2": np.asarray(r["x2"])} for r in _run(nc, im).results]
    del im, re_, rc
    gc.collect()
    out = np.zeros((B, Tfull, 1024), np.float32)
    for ci in range(NCORE):
        b, hf = ci // 2, ci % 2
        out[b, hf * TC:(hf + 1) * TC, :] = np.asarray(rf[ci]["x2"])
    return out



def build_launch_1(T=TC, TT=512):
    nc = bass.Bass("TRN2", target_bir_lowering=False, debug=False)
    x = nc.dram_tensor("x", [T, 1024], F32, kind="ExternalInput")
    x_oth = nc.dram_tensor("x_oth", [T, 1024], F32, kind="ExternalInput")
    flag = nc.dram_tensor("flag", [64, 1], F32, kind="ExternalInput")
    names = [("wkv", [1024, 1536]), ("wggk", [1024, 1536]), ("wgk2", [16, 512]),
             ("bgk2", [1, 512]), ("lt64", [64, 64]), ("wq", [1024, 512]),
             ("wk", [1024, 512]), ("wo", [1024, 1024]), ("nw8", [1, 1024]),
             ("rwt", [1024, 512]),
             ("wq1", [1024, 512]), ("wk1", [1024, 512]), ("wv1", [1024, 1024]),
             ("wsg", [1024, 1536]), ("carry_zero", [193, 1024])]
    ws = {n: nc.dram_tensor(n, shp, F32, kind="ExternalInput") for n, shp in names}
    mask8 = nc.dram_tensor("mask8", [64, 512], BF16, kind="ExternalInput")
    gate = nc.dram_tensor("gate", [8, 1024, 4096], F32, kind="ExternalInput")
    up = nc.dram_tensor("up", [8, 1024, 4096], F32, kind="ExternalInput")
    down = nc.dram_tensor("down", [8, 4096, 1024], F32, kind="ExternalInput")
    convd = nc.dram_tensor("convd", [16, 4, 128, 128], BF16, kind="ExternalInput")
    cstabs = nc.dram_tensor("cstabs", [4, 128, T], BF16, kind="ExternalInput")
    lt128 = nc.dram_tensor("lt128", [128, 128], BF16, kind="ExternalInput")
    tails_zero = nc.dram_tensor("tails_zero", [2048, 4], BF16, kind="ExternalInput")
    x1 = nc.dram_tensor("x1", [T, 1024], F32, kind="ExternalOutput")
    xnfm1 = nc.dram_tensor("xnfm1", [1024, T], BF16, kind="ExternalOutput")
    carry_out = nc.dram_tensor("carry_out", [193, 1024], F32, kind="ExternalOutput")
    tails = {nm: nc.dram_tensor(f"tails_{nm}", [shp, 4], BF16,
                                kind="ExternalOutput")
             for nm, shp in (("q", 512), ("k", 512), ("v", 1024))}
    hb1 = {}
    for n, shp in [("s_tm", [T, 256]), ("g_tm", [T, 1024]), ("qpre", [512, T]),
                   ("kpre", [512, T]), ("vpre", [1024, T])]:
        hb1[n] = nc.dram_tensor(n, shp, BF16, kind="ExternalOutput")
    hbm = {"k_tm": nc.dram_tensor("m_k_tm", [T, 512], BF16),
           "v_tm": nc.dram_tensor("m_v_tm", [T, 1024], BF16),
           "g_tm": nc.dram_tensor("m_g_tm", [T, 1024], BF16),
           "gk_tm": nc.dram_tensor("m_gk_tm", [T, 512], F32)}
    hbo = {"k_tm": nc.dram_tensor("o_k_tm", [T, 512], BF16),
           "v_tm": nc.dram_tensor("o_v_tm", [T, 1024], BF16),
           "g_tm": nc.dram_tensor("o_g_tm", [T, 1024], BF16),
           "gk_tm": nc.dram_tensor("o_gk_tm", [T, 512], F32)}
    for n, shp in [("q_fm", [512, T]), ("k_fm", [512, T]), ("v_fm", [1024, T]),
                   ("k_tm", [T, 512]), ("v_tm", [T, 1024])]:
        hb1["c" + n] = nc.dram_tensor("l1" + n, shp, BF16)
    s_oth = nc.dram_tensor("s_oth", [64, 1024], F32)
    attn0 = nc.dram_tensor("attn0", [1024, T], F32)
    xnfm0 = nc.dram_tensor("xnfm0", [1024, T], BF16)
    comb_dram = nc.dram_tensor("combd", [8, T], BF16)
    p = Prog(nc)
    c = emit_consts(p, nc)
    q_hbm = nc.dram_tensor("q_fm_s", [512, T], BF16)
    k_hbm = nc.dram_tensor("k_fm_s", [512, T], BF16)
    with ExitStack() as xs1:
        xn_fm = p.sb("xn_fm", [128, 8 * T], BF16, xs1)
        emit_ln_fm(p, x, T, xn_fm, c, "ln0", xnfm_hbm=xnfm0)
        emit_l0_proj_tm(p, ws, xn_fm, T, hbm, c)
        emit_l0_qk_fm(p, ws["wq"], ws["wk"], xn_fm, T, q_hbm, k_hbm)
        p.barrier()
    with ExitStack() as xs2:
        xn2 = p.sb("xn2_fm", [128, 8 * T], BF16, xs2)
        emit_ln_fm(p, x_oth, T, xn2, c, "lnx", xnkey="xnf2")
        emit_l0_proj_tm(p, ws, xn2, T, hbo, c, tag="l0o", xnkey="xnf2")
        p.barrier()
    emit_l0_pass1(p, hbo, T, s_oth, c, ws["lt64"], tag="p1o")
    emit_l0_pass2(p, hbm, T, q_hbm, k_hbm, s_oth, ws["wo"], ws["nw8"],
                  mask8, ws["lt64"], attn0, c, flag_dram=flag)
    p.barrier()
    with ExitStack() as fs:
        xn_fm = p.sb("xnb_fm", [128, 8 * T], BF16, fs)
        emit_load_xnfm(p, xnfm0, T, xn_fm)
        yacc = p.sb("yacc", [128, 8 * TT], F32, fs)
        hs = p.sb("hs", [128, 32 * TT], BF16, fs)
        emit_l0_router(p, ws["rwt"], xn_fm, T, comb_dram, c)
        ft = make_ffn_tiles(p, fs, min(512, TT))
        for tt in range(T // TT):
            for e in range(8):
                emit_ffn_expert(p, f"fx{e}", ft, xn_fm, T, tt, TT, up[e], down[e],
                                hs, yacc, c, w1g_dram=gate[e], comb_dram=comb_dram,
                                e_idx=e, first=(e == 0))
            p.barrier()
            emit_combine_x1(p, yacc, tt, TT, T, x, attn0, x1, c,
                            (ft["py"][0], ft["py"][1], ft["pg"][0], ft["pg"][1]),
                            tagp=f"cmb{tt}")
        p.barrier()
    with ExitStack() as ls:
        xn_fm = p.sb("xn1_fm", [128, 8 * T], BF16, ls)
        emit_ln_fm(p, x1, T, xn_fm, c, "ln1", xnfm_hbm=xnfm1)
        ws1 = {"wq": ws["wq1"], "wk": ws["wk1"], "wv": ws["wv1"], "wsg": ws["wsg"]}
        emit_l1_proj(p, ws1, xn_fm, T, hb1, c, tails_hbm=tails)
        p.barrier()
    cvh = {"qpre": hb1["qpre"], "kpre": hb1["kpre"], "vpre": hb1["vpre"],
           "q_fm": hb1["cq_fm"], "k_fm": hb1["ck_fm"], "v_fm": hb1["cv_fm"],
           "k_tm": hb1["ck_tm"], "v_tm": hb1["cv_tm"],
           "s_tm": hb1["s_tm"], "g_tm": hb1["g_tm"]}
    emit_l1_conv(p, cvh, T, convd, tails_zero, cstabs, cvh, c)
    emit_l1_scan(p, cvh, T, ws["carry_zero"], c, lt128, carry_out=carry_out,
                 full=False, tag="sp1")
    p.barrier()
    return p.finalize()


def build_launch_2(T=TC, TT=512):
    nc = bass.Bass("TRN2", target_bir_lowering=False, debug=False)
    x1 = nc.dram_tensor("x1", [T, 1024], F32, kind="ExternalInput")
    xnfm1 = nc.dram_tensor("xnfm1", [1024, T], BF16, kind="ExternalInput")
    names = [("wo", [1024, 1024]), ("gnw4", [1, 1024]), ("carry_in", [193, 1024]),
             ("rwt", [1024, 512]), ("rbp", [1, 512]),
             ("sw1", [32, 128, 8, 128]), ("sb1", [128, 32]),
             ("sw2", [8, 128, 32, 128]),
             ("sb2", [1, 1024])]
    ws = {n: nc.dram_tensor(n, shp, F32, kind="ExternalInput") for n, shp in names}
    ew1 = nc.dram_tensor("ew1", [8, 1024, 4096], F32, kind="ExternalInput")
    eb1 = nc.dram_tensor("eb1", [8, 128, 32], F32, kind="ExternalInput")
    ew2 = nc.dram_tensor("ew2", [8, 4096, 1024], F32, kind="ExternalInput")
    eb2 = nc.dram_tensor("eb2", [8, 1, 1024], F32, kind="ExternalInput")
    convd = nc.dram_tensor("convd", [16, 4, 128, 128], BF16, kind="ExternalInput")
    cstabs = nc.dram_tensor("cstabs", [4, 128, T], BF16, kind="ExternalInput")
    lt128 = nc.dram_tensor("lt128", [128, 128], BF16, kind="ExternalInput")
    mask4 = nc.dram_tensor("mask4", [128, 512], BF16, kind="ExternalInput")
    tails_in = nc.dram_tensor("tails_in", [2048, 4], BF16, kind="ExternalInput")
    hbm = {}
    for n, shp in [("s_tm", [T, 256]), ("g_tm", [T, 1024]), ("qpre", [512, T]),
                   ("kpre", [512, T]), ("vpre", [1024, T])]:
        hbm[n] = nc.dram_tensor(n, shp, BF16, kind="ExternalInput")
    for n, shp in [("q_fm", [512, T]), ("k_fm", [512, T]), ("v_fm", [1024, T]),
                   ("k_tm", [T, 512]), ("v_tm", [T, 1024])]:
        hbm[n] = nc.dram_tensor(n, shp, BF16)
    attn = nc.dram_tensor("attn", [1024, T], F32)
    x2 = nc.dram_tensor("x2", [T, 1024], F32, kind="ExternalOutput")
    comb_dram = nc.dram_tensor("combd", [8, T], BF16)
    p = Prog(nc)
    c = emit_consts(p, nc)
    emit_l1_conv(p, hbm, T, convd, tails_in, cstabs, hbm, c)
    emit_l1_scan(p, hbm, T, ws["carry_in"], c, lt128, attn_out=attn,
                 wo_hbm=ws["wo"], gnw4_dram=ws["gnw4"], mask4_dram=mask4,
                 full=True)
    p.barrier()
    with ExitStack() as fs:
        xn_fm = p.sb("xn_fm", [128, 8 * T], BF16, fs)
        emit_load_xnfm(p, xnfm1, T, xn_fm)
        yacc = p.sb("yacc", [128, 8 * TT], F32, fs)
        hs = p.sb("hs", [128, 32 * TT], BF16, fs)
        emit_l1_router(p, ws["rwt"], ws["rbp"], xn_fm, T, comb_dram, c)
        ft = make_ffn_tiles(p, fs, min(512, TT))
        for tt in range(T // TT):
            emit_ffn_expert(p, "fsh", ft, xn_fm, T, tt, TT, ws["sw1"], ws["sw2"],
                            hs, yacc, c, b1_dram=ws["sb1"], b2_dram=ws["sb2"],
                            first=True, act=AF.Gelu)
            for e in range(8):
                emit_ffn_expert(p, f"fe{e}", ft, xn_fm, T, tt, TT, ew1[e], ew2[e],
                                hs, yacc, c, b1_dram=eb1[e], b2_dram=eb2[e],
                                comb_dram=comb_dram, e_idx=e, act=AF.Gelu)
            p.barrier()
            emit_combine_x1(p, yacc, tt, TT, T, x1, attn, x2, c,
                            (ft["py"][0], ft["py"][1], ft["pg"][0], ft["pg"][1]),
                            tagp=f"cm2{tt}")
        p.barrier()
    return p.finalize()


def kernel(**ins):
    B, Tfull = 4, 8192
    import gc
    x = _f32c(ins["L0_attn_x"])
    l0w = _mk_l0_weights(ins)
    qcw = _f32c(ins["L1_attn_qconv_w"]); kcw = _f32c(ins["L1_attn_kconv_w"])
    vcw = _f32c(ins["L1_attn_vconv_w"])
    convd = np.zeros((16, 4, 128, 128), np.float32)
    for m in range(4):
        for tap in range(4):
            convd[m, tap] = np.diag(qcw[m * 128:(m + 1) * 128, 0, tap])
            convd[4 + m, tap] = np.diag(kcw[m * 128:(m + 1) * 128, 0, tap])
    for m in range(8):
        for tap in range(4):
            convd[8 + m, tap] = np.diag(vcw[m * 128:(m + 1) * 128, 0, tap])
    convd = convd.astype(BF)
    inv = 1.0 / 10000.0 ** (np.arange(0, 128, 2, dtype=np.float64) / 128)
    scale = 128.0 ** -0.5
    cstabs_h = []
    for hf in range(2):
        pos0 = hf * TC
        frq = (pos0 + np.arange(TC, dtype=np.float64))[:, None] * inv[None, :]
        emb = np.concatenate([frq, frq], -1)
        cstabs_h.append(np.ascontiguousarray(np.stack(
            [np.cos(emb).T * scale, np.sin(emb).T * scale,
             np.cos(emb).T, np.sin(emb).T]).astype(np.float32).astype(BF)))
    R = np.zeros((128, 128), np.float32)
    R[:64, 64:] = -np.eye(64); R[64:, :64] = np.eye(64)
    ropem = np.ascontiguousarray(R.T).astype(BF)
    lt128 = np.triu(np.ones((128, 128), np.float32)).astype(BF)
    mask4 = np.ascontiguousarray(np.tile(
        np.triu(np.ones((128, 128), np.float32)), (1, 4))).astype(BF)
    gnw4 = np.ascontiguousarray(np.tile(_f32c(ins["L1_attn_gnorm_w"]), 4))[None, :]
    wsg = np.ascontiguousarray(np.concatenate(
        [_f32c(ins["L1_attn_Ws"]), np.zeros((1024, 256), np.float32),
         _f32c(ins["L1_attn_Wg"])], 1))
    rwt0 = np.ascontiguousarray(np.concatenate(
        [_f32c(ins["L0_ffn_router_w"]).T, np.zeros((1024, 504), np.float32)], 1))
    rwt1 = np.ascontiguousarray(np.concatenate(
        [_f32c(ins["L1_ffn_router_w"]), np.zeros((1024, 504), np.float32)], 1))
    rbp1 = np.ascontiguousarray(np.concatenate(
        [_f32c(ins["L1_ffn_router_b"]), np.zeros(504, np.float32)]))[None, :]
    sb1 = np.ascontiguousarray(_f32c(ins["L1_ffn_shared_b1"]).reshape(32, 128).T)
    eb1 = np.ascontiguousarray(_f32c(ins["L1_ffn_exp_b1"]).reshape(8, 32, 128)
                               .transpose(0, 2, 1))
    zero_carry = np.zeros((193, 1024), np.float32)
    zero_tails = np.zeros((2048, 4), np.float32).astype(BF)
    flag_arr = [np.full((64, 1), float(ci % 2), np.float32) for ci in range(NCORE)]
    shards = [np.ascontiguousarray(x[ci // 2, (ci % 2) * TC:(ci % 2 + 1) * TC, :])
              for ci in range(NCORE)]
    gate_t = np.ascontiguousarray(_f32c(ins["L0_ffn_gate_w"]).transpose(0, 2, 1))
    up_t = np.ascontiguousarray(_f32c(ins["L0_ffn_up_w"]).transpose(0, 2, 1))
    down_t = np.ascontiguousarray(_f32c(ins["L0_ffn_down_w"]).transpose(0, 2, 1))
    nc = build_launch_1()
    im = [dict(x=shards[ci], x_oth=shards[ci ^ 1], flag=flag_arr[ci],
               wkv=l0w["wkv"], wggk=l0w["wggk"], wgk2=l0w["wgk2"],
               bgk2=l0w["bgk2"], lt64=l0w["lt64"], wq=l0w["wq"], wk=l0w["wk"],
               wo=l0w["wo"], nw8=l0w["nw8"], mask8=l0w["mask8"], rwt=rwt0,
               gate=gate_t, up=up_t, down=down_t,
               wq1=_f32c(ins["L1_attn_Wq"]), wk1=_f32c(ins["L1_attn_Wk"]),
               wv1=_f32c(ins["L1_attn_Wv"]), wsg=wsg, carry_zero=zero_carry,
               convd=convd, cstabs=cstabs_h[ci % 2], lt128=lt128,
               tails_zero=zero_tails, rope_mat=ropem,
               identb=l0w["identb"], identf=l0w["identf"])
          for ci in range(NCORE)]
    r1 = [{k2: np.asarray(v2) for k2, v2 in r.items()}
          for r in _run(nc, im).results]
    del im, gate_t, up_t, down_t
    gc.collect()
    nc = build_launch_2()
    im = []
    for ci in range(NCORE):
        if ci % 2 == 0:
            carry, tails = zero_carry, zero_tails
        else:
            carry = _f32c(r1[ci - 1]["carry_out"])
            tails = np.ascontiguousarray(np.concatenate(
                [r1[ci - 1]["tails_q"], r1[ci - 1]["tails_k"],
                 r1[ci - 1]["tails_v"]], 0))
        im.append(dict(x1=_f32c(r1[ci]["x1"]), xnfm1=r1[ci]["xnfm1"],
                       wo=_f32c(ins["L1_attn_Wo"]), gnw4=gnw4, carry_in=carry,
                       rwt=rwt1, rbp=rbp1,
                       sw1=_f32c(ins["L1_ffn_shared_w1"]), sb1=sb1,
                       sw2=_f32c(ins["L1_ffn_shared_w2"]),
                       sb2=_f32c(ins["L1_ffn_shared_b2"])[None, :],
                       ew1=_f32c(ins["L1_ffn_exp_w1"]), eb1=eb1,
                       ew2=_f32c(ins["L1_ffn_exp_w2"]),
                       eb2=_f32c(ins["L1_ffn_exp_b2"])[:, None, :],
                       convd=convd, cstabs=cstabs_h[ci % 2], lt128=lt128,
                       mask4=mask4, tails_in=tails,
                       s_tm=r1[ci]["s_tm"], g_tm=r1[ci]["g_tm"],
                       qpre=r1[ci]["qpre"], kpre=r1[ci]["kpre"],
                       vpre=r1[ci]["vpre"], rope_mat=ropem,
                       identb=l0w["identb"], identf=l0w["identf"]))
    r2 = [{"x2": np.asarray(r["x2"])} for r in _run(nc, im).results]
    del im, r1
    gc.collect()
    out = np.zeros((B, Tfull, 1024), np.float32)
    for ci in range(NCORE):
        out[ci // 2, (ci % 2) * TC:(ci % 2 + 1) * TC, :] = r2[ci]["x2"]
    return out



def emit_xn_tile_load(p, xnfm_hbm, T, TILE, g, xt, xnkey):
    for db in range(8):
        p.op("sync", lambda s, db=db, g=g: s.dma_start(
            out=xt[:, db * TILE:(db + 1) * TILE],
            in_=xnfm_hbm[db * 128:(db + 1) * 128, g * TILE:(g + 1) * TILE]),
            writes=[(xnkey, db)], dma=f"xnt{g % 2}")


def emit_selcopy(p, dst, src, selc_dram, tag, colofs=False, TH=4096):
    R, C = dst.shape[0], dst.shape[1]
    fl = p.sb(f"{tag}_f", [128, 2], F32)
    p.op("sync", lambda s: s.dma_start(out=fl[:, :], in_=selc_dram[:, :]),
         writes=[(tag, "f")], dma="dself")
    with ExitStack() as sc:
        W = 512
        ta = [p.sb(f"{tag}_a{j}", [128, W], F32 if dst.dtype == F32 else BF16, sc)
              for j in range(2)]
        tb_ = [p.sb(f"{tag}_b{j}", [128, W], F32 if dst.dtype == F32 else BF16, sc)
               for j in range(2)]
        to = [p.sb(f"{tag}_o{j}", [128, W], F32 if dst.dtype == F32 else BF16, sc)
              for j in range(2)]
        for r0 in range(0, R, 128):
            for c0 in range(0, C, W):
                j = ((r0 // 128) * (C // W) + c0 // W) % 2
                ka, kb, ko = (tag, "a", j), (tag, "b", j), (tag, "o", j)
                if colofs:
                    sa = src[r0:r0 + 128, c0:c0 + W]
                    sb_ = src[r0:r0 + 128, TH + c0:TH + c0 + W]
                else:
                    sa = src[r0:r0 + 128, c0:c0 + W]
                    sb_ = src[TH + r0:TH + r0 + 128, c0:c0 + W]
                p.op("sync", lambda s, sa=sa, j=j: s.dma_start(
                    out=ta[j][:, :], in_=sa), writes=[ka], dma=f"dsa{j}")
                p.op("sync", lambda s, sb_=sb_, j=j: s.dma_start(
                    out=tb_[j][:, :], in_=sb_), writes=[kb], dma=f"dsb{j}")
                p.op("vector", lambda v, j=j: v.tensor_scalar_mul(
                    to[j][:, :], ta[j][:, :], fl[:, 0:1]),
                    reads=[ka, (tag, "f")], writes=[ko])
                p.op("vector", lambda v, j=j: v.scalar_tensor_tensor(
                    to[j][:, :], tb_[j][:, :], fl[:, 1:2], to[j][:, :],
                    ALU.mult, ALU.add),
                    reads=[kb, ko, (tag, "f")], writes=[ko])
                p.op("sync", lambda s, r0=r0, c0=c0, j=j: s.dma_start(
                    out=dst[r0:r0 + 128, c0:c0 + W], in_=to[j][:, :]),
                    reads=[ko], dma=f"dso{j}")
    p.barrier()


def build_fused(TB=8192, TH=4096, TT=512, TT2=1024):
    nc = bass.Bass("TRN2", target_bir_lowering=False, debug=False)
    x = nc.dram_tensor("x", [TB, 1024], F32, kind="ExternalInput")
    selc = nc.dram_tensor("selc", [128, 2], F32, kind="ExternalInput")
    names = [("wkv", [1024, 1536]), ("wggk", [1024, 1536]), ("wgk2", [16, 512]),
             ("bgk2", [1, 512]), ("lt64", [64, 64]), ("wq", [1024, 512]),
             ("wk", [1024, 512]), ("wo", [1024, 1024]), ("nw8", [1, 1024]),
             ("rwt", [1024, 512]),
             ("wq1", [1024, 512]), ("wk1", [1024, 512]), ("wv1", [1024, 1024]),
             ("wsg", [1024, 1536]), ("carry_zero", [193, 1024]),
             ("wo1", [1024, 1024]), ("gnw4", [1, 1024]),
             ("rwt1", [1024, 512]), ("rbp", [1, 512]),
             ("sw1", [32, 128, 8, 128]), ("sb1", [128, 32]),
             ("sw2", [8, 128, 32, 128]),
             ("sb2", [1, 1024]), ("s_zero", [64, 1024])]
    ws = {n: nc.dram_tensor(n, shp, F32, kind="ExternalInput") for n, shp in names}
    mask8 = nc.dram_tensor("mask8", [64, 512], BF16, kind="ExternalInput")
    gate = nc.dram_tensor("gate", [8, 32, 128, 8, 128], F32, kind="ExternalInput")
    up = nc.dram_tensor("up", [8, 32, 128, 8, 128], F32, kind="ExternalInput")
    down = nc.dram_tensor("down", [8, 8, 128, 32, 128], F32, kind="ExternalInput")
    ew1 = nc.dram_tensor("ew1", [8, 32, 128, 8, 128], F32, kind="ExternalInput")
    eb1 = nc.dram_tensor("eb1", [8, 128, 32], F32, kind="ExternalInput")
    ew2 = nc.dram_tensor("ew2", [8, 8, 128, 32, 128], F32, kind="ExternalInput")
    eb2 = nc.dram_tensor("eb2", [8, 1, 1024], F32, kind="ExternalInput")
    convd = nc.dram_tensor("convd", [16, 4, 128, 128], BF16, kind="ExternalInput")
    cstabs = nc.dram_tensor("cstabs", [4, 128, TB], BF16, kind="ExternalInput")
    lt128 = nc.dram_tensor("lt128", [128, 128], BF16, kind="ExternalInput")
    mask4 = nc.dram_tensor("mask4", [128, 512], BF16, kind="ExternalInput")
    tails_zero = nc.dram_tensor("tails_zero", [2048, 4], BF16, kind="ExternalInput")
    x2 = nc.dram_tensor("x2", [TH, 1024], F32, kind="ExternalOutput")
    xnfm0 = nc.dram_tensor("xnfm0", [1024, TB], BF16)
    xnfm1 = nc.dram_tensor("xnfm1", [1024, TB], BF16)
    q_hbm = nc.dram_tensor("q_fm_s", [512, TB], BF16)
    k_hbm = nc.dram_tensor("k_fm_s", [512, TB], BF16)
    hbm = {"k_tm": nc.dram_tensor("m_k_tm", [TB, 512], BF16),
           "v_tm": nc.dram_tensor("m_v_tm", [TB, 1024], BF16),
           "g_tm": nc.dram_tensor("m_g_tm", [TB, 1024], BF16),
           "gk_tm": nc.dram_tensor("m_gk_tm", [TB, 512], F32)}
    attn0 = nc.dram_tensor("attn0", [1024, TB], F32)
    attn1 = nc.dram_tensor("attn1", [1024, TB], F32)
    x1 = nc.dram_tensor("x1", [TB, 1024], F32)
    comb0 = nc.dram_tensor("comb0", [8, TB], BF16)
    comb1 = nc.dram_tensor("comb1", [8, TH], BF16)
    cv = {}
    for n, shp in [("s_tm", [TB, 256]), ("g_tm", [TB, 1024]), ("qpre", [512, TB]),
                   ("kpre", [512, TB]), ("vpre", [1024, TB]),
                   ("q_fm", [512, TB]), ("k_fm", [512, TB]), ("v_fm", [1024, TB]),
                   ("k_tm", [TB, 512]), ("v_tm", [TB, 1024])]:
        cv[n] = nc.dram_tensor("l1" + n, shp, BF16)
    x1_own = nc.dram_tensor("x1_own", [TH, 1024], F32)
    attn1_own = nc.dram_tensor("attn1_own", [1024, TH], F32)
    xn1_own = nc.dram_tensor("xn1_own", [1024, TH], BF16)

    p = Prog(nc)
    c = emit_consts(p, nc)
    NG = TB // TT
    emit_ln_fm(p, x, TB, None, c, "ln0", xnfm_hbm=xnfm0)
    p.barrier()
    with ExitStack() as sc:
        xt = [p.sb(f"fx_xt{j}", [128, 8 * TT], BF16, sc) for j in range(2)]
        qkb = make_qk_bufs(p, sc)
        for g in range(NG):
            j = g % 2
            xnkey = f"xn{j}"
            emit_xn_tile_load(p, xnfm0, TB, TT, g, xt[j], xnkey)
            sl = slice(g * TT, (g + 1) * TT)
            hsl = {k2: v2[sl, :] for k2, v2 in hbm.items()}
            emit_l0_proj_tm(p, ws, xt[j], TT, hsl, c, tag=f"lp{g}", xnkey=xnkey)
            emit_l0_qk_fm(p, ws["wq"], ws["wk"], xt[j], TT,
                          q_hbm[:, sl], k_hbm[:, sl], tagp=f"qk{g}", xnkey=xnkey,
                          bufs=qkb)
            emit_l0_router(p, ws["rwt"], xt[j], TT, comb0[:, sl], c,
                           tagp=f"rt{g}", xnkey=xnkey)
    p.barrier()
    emit_l0_pass2(p, hbm, TB, q_hbm, k_hbm, ws["s_zero"], ws["wo"], ws["nw8"],
                  mask8, ws["lt64"], attn0, c)
    p.barrier()
    with ExitStack() as fs:
        xt = [p.sb(f"ff_xt{j}", [128, 8 * TT2], BF16, fs) for j in range(2)]
        yacc = p.sb("yacc", [128, 8 * TT2], F32, fs)
        hs = p.sb("hs", [128, 32 * TT2], BF16, fs)
        ft = make_ffn_tiles(p, fs, min(512, TT2))
        for g in range(TB // TT2):
            j = g % 2
            xnkey = f"xf{j}"
            emit_xn_tile_load(p, xnfm0, TB, TT2, g, xt[j], xnkey)
            sl = slice(g * TT2, (g + 1) * TT2)
            for e in range(8):
                emit_ffn_expert(p, f"f{e}", ft, xt[j], TT2, 0, TT2, up[e], down[e],
                                hs, yacc, c, w1g_dram=gate[e],
                                comb_dram=comb0[:, sl], e_idx=e, first=(e == 0),
                                xnkey=xnkey)
            p.barrier()
            emit_combine_x1(p, yacc, 0, TT2, TT2, x[sl, :], attn0[:, sl],
                            x1[sl, :], c,
                            (ft["py"][0], ft["py"][1], ft["pg"][0], ft["pg"][1]),
                            tagp=f"cb{g}")
    p.barrier()
    emit_ln_fm(p, x1, TB, None, c, "ln1", xnfm_hbm=xnfm1)
    p.barrier()
    ws1 = {"wq": ws["wq1"], "wk": ws["wk1"], "wv": ws["wv1"], "wsg": ws["wsg"]}
    with ExitStack() as sc:
        xt = [p.sb(f"l1_xt{j}", [128, 8 * TT], BF16, sc) for j in range(2)]
        l1b = {"wsb": p.sb("l1b_w", [128, 8 * 128], BF16, sc),
               "oq": [p.sb(f"l1b_o{j}", [128, 512], BF16, sc) for j in range(2)]}
        for g in range(NG):
            j = g % 2
            xnkey = f"xl{j}"
            emit_xn_tile_load(p, xnfm1, TB, TT, g, xt[j], xnkey)
            sl = slice(g * TT, (g + 1) * TT)
            csl = {"s_tm": cv["s_tm"][sl, :], "g_tm": cv["g_tm"][sl, :],
                   "qpre": cv["qpre"][:, sl], "kpre": cv["kpre"][:, sl],
                   "vpre": cv["vpre"][:, sl]}
            emit_l1_proj(p, ws1, xt[j], TT, csl, c, xnkey=xnkey, tag=f"lq{g}",
                         bufs=l1b)
    p.barrier()
    emit_l1_conv(p, cv, TB, convd, tails_zero, cstabs, cv, c)
    emit_l1_scan(p, cv, TB, ws["carry_zero"], c, lt128, attn_out=attn1,
                 wo_hbm=ws["wo1"], gnw4_dram=ws["gnw4"], mask4_dram=mask4,
                 full=True)
    p.barrier()
    emit_selcopy(p, x1_own, x1, selc, "dx1", TH=TH)
    emit_selcopy(p, attn1_own, attn1, selc, "da1", colofs=True, TH=TH)
    emit_selcopy(p, xn1_own, xnfm1, selc, "dxn", colofs=True, TH=TH)
    p.barrier()
    with ExitStack() as fs:
        xt = [p.sb(f"f2_xt{j}", [128, 8 * TT2], BF16, fs) for j in range(2)]
        yacc = p.sb("yacc2", [128, 8 * TT2], F32, fs)
        hs = p.sb("hs2", [128, 32 * TT2], BF16, fs)
        for g in range(TH // TT2):
            j = g % 2
            xnkey = f"x2{j}"
            emit_xn_tile_load(p, xn1_own, TH, TT2, g, xt[j], xnkey)
            sl = slice(g * TT2, (g + 1) * TT2)
            emit_l1_router(p, ws["rwt1"], ws["rbp"], xt[j], TT2, comb1[:, sl], c,
                           tagp=f"r1{g}", xnkey=xnkey)
        p.barrier()
        ft = make_ffn_tiles(p, fs, min(512, TT2), tag="ff2")
        for g in range(TH // TT2):
            j = g % 2
            xnkey = f"x2{j}"
            emit_xn_tile_load(p, xn1_own, TH, TT2, g, xt[j], xnkey)
            sl = slice(g * TT2, (g + 1) * TT2)
            emit_ffn_expert(p, "fsh", ft, xt[j], TT2, 0, TT2, ws["sw1"], ws["sw2"],
                            hs, yacc, c, b1_dram=ws["sb1"], b2_dram=ws["sb2"],
                            first=True, act=AF.Gelu, xnkey=xnkey)
            for e in range(8):
                emit_ffn_expert(p, f"g{e}", ft, xt[j], TT2, 0, TT2, ew1[e], ew2[e],
                                hs, yacc, c, b1_dram=eb1[e], b2_dram=eb2[e],
                                comb_dram=comb1[:, sl], e_idx=e, act=AF.Gelu,
                                xnkey=xnkey)
            p.barrier()
            emit_combine_x1(p, yacc, 0, TT2, TT2, x1_own[sl, :], attn1_own[:, sl],
                            x2[sl, :], c,
                            (ft["py"][0], ft["py"][1], ft["pg"][0], ft["pg"][1]),
                            tagp=f"c2{g}")
    p.barrier()
    return p.finalize()


def kernel_fused(**ins):
    B, Tfull = 4, 8192
    import gc
    x = _f32c(ins["L0_attn_x"])
    l0w = _mk_l0_weights(ins)
    qcw = _f32c(ins["L1_attn_qconv_w"]); kcw = _f32c(ins["L1_attn_kconv_w"])
    vcw = _f32c(ins["L1_attn_vconv_w"])
    convd = np.zeros((16, 4, 128, 128), np.float32)
    for m in range(4):
        for tap in range(4):
            convd[m, tap] = np.diag(qcw[m * 128:(m + 1) * 128, 0, tap])
            convd[4 + m, tap] = np.diag(kcw[m * 128:(m + 1) * 128, 0, tap])
    for m in range(8):
        for tap in range(4):
            convd[8 + m, tap] = np.diag(vcw[m * 128:(m + 1) * 128, 0, tap])
    convd = convd.astype(BF)
    inv = 1.0 / 10000.0 ** (np.arange(0, 128, 2, dtype=np.float64) / 128)
    scale = 128.0 ** -0.5
    frq = np.arange(Tfull, dtype=np.float64)[:, None] * inv[None, :]
    emb = np.concatenate([frq, frq], -1)
    cstabs_h = np.ascontiguousarray(np.stack(
        [np.cos(emb).T * scale, np.sin(emb).T * scale,
         np.cos(emb).T, np.sin(emb).T]).astype(np.float32).astype(BF))
    R = np.zeros((128, 128), np.float32)
    R[:64, 64:] = -np.eye(64); R[64:, :64] = np.eye(64)
    ropem = np.ascontiguousarray(R.T).astype(BF)
    lt128 = np.triu(np.ones((128, 128), np.float32)).astype(BF)
    mask4 = np.ascontiguousarray(np.tile(
        np.triu(np.ones((128, 128), np.float32)), (1, 4))).astype(BF)
    gnw4 = np.ascontiguousarray(np.tile(_f32c(ins["L1_attn_gnorm_w"]), 4))[None, :]
    wsg = np.ascontiguousarray(np.concatenate(
        [_f32c(ins["L1_attn_Ws"]), np.zeros((1024, 256), np.float32),
         _f32c(ins["L1_attn_Wg"])], 1))
    rwt0 = np.ascontiguousarray(np.concatenate(
        [_f32c(ins["L0_ffn_router_w"]).T, np.zeros((1024, 504), np.float32)], 1))
    rwt1 = np.ascontiguousarray(np.concatenate(
        [_f32c(ins["L1_ffn_router_w"]), np.zeros((1024, 504), np.float32)], 1))
    rbp1 = np.ascontiguousarray(np.concatenate(
        [_f32c(ins["L1_ffn_router_b"]), np.zeros(504, np.float32)]))[None, :]
    sb1 = np.ascontiguousarray(_f32c(ins["L1_ffn_shared_b1"]).reshape(32, 128).T)
    eb1 = np.ascontiguousarray(_f32c(ins["L1_ffn_exp_b1"]).reshape(8, 32, 128)
                               .transpose(0, 2, 1))
    def prep_w1(w_t):
        return np.ascontiguousarray(
            w_t.reshape(-1, 8, 128, 32, 128).transpose(0, 3, 2, 1, 4))

    def prep_w2(w_t):
        return np.ascontiguousarray(
            w_t.reshape(-1, 32, 128, 8, 128).transpose(0, 3, 2, 1, 4))
    gate_t = prep_w1(_f32c(ins["L0_ffn_gate_w"]).transpose(0, 2, 1))
    up_t = prep_w1(_f32c(ins["L0_ffn_up_w"]).transpose(0, 2, 1))
    down_t = prep_w2(_f32c(ins["L0_ffn_down_w"]).transpose(0, 2, 1))
    ew1_t = prep_w1(_f32c(ins["L1_ffn_exp_w1"]))
    ew2_t = prep_w2(_f32c(ins["L1_ffn_exp_w2"]))
    sw1_t = prep_w1(_f32c(ins["L1_ffn_shared_w1"])[None])[0]
    sw2_t = prep_w2(_f32c(ins["L1_ffn_shared_w2"])[None])[0]
    nc = build_fused()
    im = [dict(x=np.ascontiguousarray(x[ci // 2]),
               selc=np.ascontiguousarray(np.tile(np.array(
                   [[1.0 - ci % 2, float(ci % 2)]], np.float32), (128, 1))),
               wkv=l0w["wkv"], wggk=l0w["wggk"], wgk2=l0w["wgk2"],
               bgk2=l0w["bgk2"], lt64=l0w["lt64"], wq=l0w["wq"], wk=l0w["wk"],
               wo=l0w["wo"], nw8=l0w["nw8"], mask8=l0w["mask8"], rwt=rwt0,
               gate=gate_t, up=up_t, down=down_t,
               wq1=_f32c(ins["L1_attn_Wq"]), wk1=_f32c(ins["L1_attn_Wk"]),
               wv1=_f32c(ins["L1_attn_Wv"]), wsg=wsg,
               carry_zero=np.zeros((193, 1024), np.float32),
               s_zero=np.zeros((64, 1024), np.float32),
               wo1=_f32c(ins["L1_attn_Wo"]), gnw4=gnw4, rwt1=rwt1, rbp=rbp1,
               sw1=sw1_t, sb1=sb1, sw2=sw2_t,
               sb2=_f32c(ins["L1_ffn_shared_b2"])[None, :],
               ew1=ew1_t, eb1=eb1, ew2=ew2_t,
               eb2=_f32c(ins["L1_ffn_exp_b2"])[:, None, :],
               convd=convd, cstabs=cstabs_h, lt128=lt128, mask4=mask4,
               tails_zero=np.zeros((2048, 4), np.float32).astype(BF),
               rope_mat=ropem, identb=l0w["identb"], identf=l0w["identf"])
          for ci in range(NCORE)]
    rr = [{"x2": np.asarray(r["x2"])} for r in _run(nc, im).results]
    del im
    gc.collect()
    out = np.zeros((B, Tfull, 1024), np.float32)
    for ci in range(NCORE):
        out[ci // 2, (ci % 2) * TC:(ci % 2 + 1) * TC, :] = rr[ci]["x2"]
    return out


kernel = kernel_fused


def kernel_prep_w1(w_t):
    return np.ascontiguousarray(
        w_t.reshape(-1, 8, 128, 32, 128).transpose(0, 3, 2, 1, 4))


def kernel_prep_w2(w_t):
    return np.ascontiguousarray(
        w_t.reshape(-1, 32, 128, 8, 128).transpose(0, 3, 2, 1, 4))
